# version 18
# baseline (speedup 1.0000x reference)
import contextlib
import numpy as np

import concourse.bass as bass
import concourse.mybir as mybir
from concourse.bass_utils import run_bass_kernel_spmd

F32 = mybir.dt.float32
BF16 = mybir.dt.bfloat16
I32 = mybir.dt.int32
ALU = mybir.AluOpType
ACT = mybir.ActivationFunctionType
AX = mybir.AxisListType

NCORES = 8
D = 1024
F = 4096
T = 4096
NT = 8
TW = 512
KC = D // 128
FC = F // 128
H = 16
LN_EPS = 1e-5
ATT_EPS = 1e-6
R = 16


class Sem:
    def __init__(self, handle):
        self.h = handle
        self.n = 0


class Prog:

    def __init__(self):
        self.items = []
        self.waited = {}
        self.chain_sems = {}

    def op(self, eng, builder, waits=(), inc=None, inc_k=1, chain=None):
        cs = self.chain_sems.get(eng)
        if chain is None:
            chain = cs is not None
        waits = list(waits)
        if chain and cs is not None:
            waits.append((cs, cs.n))
            if inc is None:
                inc, inc_k = cs, 1
            else:
                assert inc is cs, f"chained op on {eng} must inc its chain sem"
        cap = []
        for s, v in waits:
            if v <= 0:
                continue
            key = (eng, id(s))
            if self.waited.get(key, -1) >= v:
                continue
            self.waited[key] = v
            cap.append((s.h, v))
        tgt = None
        if inc is not None:
            inc.n += inc_k
            tgt = (inc.h, inc_k)

        def fn(e, _cap=cap, _b=builder, _tgt=tgt):
            for h, v in _cap:
                e.wait_ge(h, v)
            ins = _b(e)
            if _tgt is not None:
                ins.then_inc(*_tgt)
        self.items.append((eng, fn))

    def replay(self, eng_name, e):
        for name, fn in self.items:
            if name == eng_name:
                fn(e)


DEBUG = False
PHASES = 4


def build_nc(debug=None, phases=None):
    if debug is None:
        debug = DEBUG
    if phases is None:
        phases = PHASES
    nc = bass.Bass("TRN2", target_bir_lowering=False)

    dr = lambda name, shape, dt=F32: nc.dram_tensor(name, shape, dt, kind="ExternalInput").ap()
    x_in = dr("x_t", [D, T])
    wfc_t = dr("wfc_t", [D, F])
    wpj_t = dr("wpj_t", [F, D])
    afc_in = dr("a_fc", [R, D])
    bfc2_in = dr("b_fc_t", [R, F])
    apj_in = dr("a_pj", [R, F])
    bpj2_in = dr("b_pj_t", [R, D])
    wq_in = dr("wq", [D, D])
    wk_in = dr("wk", [D, D])
    wv_in = dr("wv", [D, D])
    wo_in = dr("wo", [D, D])
    bfc_in = dr("bias_fc", [FC, 128])
    bpj_in = dr("bias_pj", [KC, 128])
    bo_in = dr("bias_o", [KC, 128])
    bq_in = dr("bias_q", [KC, 128])
    bk_in = dr("bias_k", [1, D])
    bv_in = dr("bias_v", [1, D])
    eye_in = dr("eye", [128, 128])
    ebc_in = dr("e_bc", [H, KC * 128])

    out_x = nc.dram_tensor("out_x", [D, T], F32, kind="ExternalOutput").ap()

    scr_fc = nc.dram_tensor("scr_fc", [FC, 128, KC * 128], BF16).ap()
    scr_pj = nc.dram_tensor("scr_pj", [KC, 128, FC * 128], BF16).ap()
    ln2_dram = nc.dram_tensor("ln2_dram", [2, T], F32).ap()
    ln3_dram = nc.dram_tensor("ln3_dram", [2, T], F32).ap()
    cc_in = nc.dram_tensor("cc_in", [128, 520], F32)
    cc_out = nc.dram_tensor("cc_out", [128, 520], F32)
    if debug:
        dbg_x1 = nc.dram_tensor("dbg_x1", [D, T], F32, kind="ExternalOutput").ap()
        dbg_x2 = nc.dram_tensor("dbg_x2", [D, T], F32, kind="ExternalOutput").ap()
        dbg_kv = nc.dram_tensor("dbg_kv", [128, 520], F32, kind="ExternalOutput").ap()
        dbg_wfc = nc.dram_tensor("dbg_wfc", [FC, 128, KC * 128], BF16,
                                 kind="ExternalOutput").ap()
        dbg_wpj = nc.dram_tensor("dbg_wpj", [KC, 128, FC * 128], BF16,
                                 kind="ExternalOutput").ap()
        dbg_ln2 = nc.dram_tensor("dbg_ln2", [2, T], F32, kind="ExternalOutput").ap()
        dbg_qf = nc.dram_tensor("dbg_qf", [128, KC * TW], BF16, kind="ExternalOutput").ap()
        dbg_z = nc.dram_tensor("dbg_z", [H, TW], F32, kind="ExternalOutput").ap()
        dbg_y = nc.dram_tensor("dbg_y", [128, KC * TW], BF16, kind="ExternalOutput").ap()
        dbg_ln5 = nc.dram_tensor("dbg_ln5", [128, KC * TW], BF16, kind="ExternalOutput").ap()

    P = Prog()

    with contextlib.ExitStack() as ctx:
        avail = nc.sbuf_bytes_remaining
        words = (avail - 1024) // 4
        arena = ctx.enter_context(nc.sbuf_tensor("arena", [128, words], F32))
        _off = [0]

        def reg(nbytes, align=4):
            a = (_off[0] + align - 1) // align * align
            _off[0] = a + nbytes
            return a

        def f32v(off, n, p0=0, pn=128):
            assert off % 4 == 0
            return arena[p0:p0 + pn, off // 4: off // 4 + n]

        def bf16v(off, n, p0=0, pn=128):
            assert off % 4 == 0 and n % 2 == 0
            return arena[p0:p0 + pn, off // 4: off // 4 + n // 2].bitcast(BF16)

        def i32v(off, n, p0=0, pn=128):
            return arena[p0:p0 + pn, off // 4: off // 4 + n].bitcast(I32)

        pb = [ctx.enter_context(nc.psum_tensor(f"pb{i}", [128, 512], F32)) for i in range(8)]

        mk = lambda name: Sem(ctx.enter_context(nc.semaphore(name)))
        sT = mk("sT"); sV = mk("sV"); sS = mk("sS"); sG = mk("sG")
        sCC = mk("sCC"); dOUT = mk("dOUT")
        sPRO = mk("sPRO")
        sSTGl = [mk("sSTGa"), mk("sSTGb")]
        sX = mk("sX")
        sSCW = mk("sSCW")
        dWa = mk("dWa"); dWb = mk("dWb"); dWc = mk("dWc")
        dPa = mk("dPa"); dPb = mk("dPb")
        sROW = mk("sROW")
        sRST = mk("sRST")
        gW = mk("gW")
        gEBC = mk("gEBC")
        gCC = mk("gCC")
        gOUT = mk("gOUT")
        sXl = [mk(f"sX{i}") for i in range(KC)]
        P.chain_sems = {"vector": sV, "scalar": sS, "gpsimd": sG}

        o_eye = reg(128 * 4)
        o_bfc = reg(FC * 4)
        o_bpj = reg(KC * 4)
        o_bpjh = reg(KC * 4)
        o_bo = reg(KC * 4)
        o_bq = reg(KC * 4)
        o_ones_cbf = reg(8)
        o_ones_rbf = reg(128 * 2)
        o_ones_rf = reg(128 * 4)
        o_sc = reg(8)
        o_inv = reg(8)
        o_half = reg(8)
        o_zero = reg(8)
        o_amaxc = reg(8)
        o_srow = reg(128 * 4)
        o_rows = reg(4 * 512 * 4)
        o_btmp = reg(128 * 4)

        eye = f32v(o_eye, 128)
        bfc_col = f32v(o_bfc, FC)
        bpj_col = f32v(o_bpj, KC)
        bpjh_col = f32v(o_bpjh, KC)
        bo_col = f32v(o_bo, KC)
        bq_col = f32v(o_bq, KC)
        ones_cbf = bf16v(o_ones_cbf, 2)[:, 0:1]
        ones_rbf = bf16v(o_ones_rbf, 128, 0, 1)
        ones_rf = f32v(o_ones_rf, 128, 0, 1)
        sc_col = f32v(o_sc, 1)
        inv_col = f32v(o_inv, 1)
        half_col = f32v(o_half, 1)
        zero_col = f32v(o_zero, 1)
        amax_col = f32v(o_amaxc, 1)
        srow = f32v(o_srow, 128, 0, 1)
        rowm = f32v(o_rows, 512, 0, 1)
        rowv = f32v(o_rows + 2048, 512, 0, 1)
        rowa = f32v(o_rows + 4096, 512, 0, 1)
        rowb = f32v(o_rows + 6144, 512, 0, 1)

        o_dyn = _off[0]

        o_stage = o_dyn
        o_weff = o_stage + 2 * 8192 * 4
        o_qt = o_weff + 32768 * 2
        o_qq = o_qt + 2048 * 4
        o_qc = o_qq + 2048 * 4
        o_afc = o_qc + 2048 * 4
        o_apj = o_afc + D * 2
        o_b2fc = o_apj + F * 2
        o_b2pj = o_b2fc + F * 2
        o_qt2 = o_b2pj + D * 2
        o_qq2 = o_qt2 + 2048 * 4
        o_qc2 = o_qq2 + 2048 * 4
        pro_end = o_qc2 + 2048 * 4

        stage = lambda w0, n, h=0: f32v(o_stage + (h * 8192 + w0) * 4, n)
        weff_flat = bf16v(o_weff, 32768)
        qset = [(f32v(o_qt, 2048), f32v(o_qq, 2048), f32v(o_qc, 2048),
                 i32v(o_qc, 2048), o_qt),
                (f32v(o_qt2, 2048), f32v(o_qq2, 2048), f32v(o_qc2, 2048),
                 i32v(o_qc2, 2048), o_qt2)]
        afc_bf = bf16v(o_afc, D, 0, R)
        apj_bf = bf16v(o_apj, F, 0, R)
        b2fc = bf16v(o_b2fc, F, 0, R)
        b2pj = bf16v(o_b2pj, D, 0, R)
        b2st_fc = f32v(o_qt2, F, 0, R)
        b2st_pj = f32v(o_qt2, D, 0, R)

        o_X = o_dyn
        o_h = o_X + 32768 * 4
        o_ln = o_h + FC * TW * 2
        o_t1 = o_ln + KC * TW * 2
        o_xbf = o_t1 + 512 * 4
        o_x2bf = o_xbf + 1024 * 2
        o_wbuf = o_x2bf + 1024 * 2
        o_wpbuf = o_wbuf + 2048 * 3
        o_lnt1 = o_xbf
        main_end = o_wpbuf + 8192 * 2

        Xv = lambda c, t0, n: f32v(o_X + (c * T + t0) * 4, n)
        hv = lambda o: bf16v(o_h + o * TW * 2, TW)
        lnv = lambda c, j0, n: bf16v(o_ln + (c * TW + j0) * 2, n)
        t1v = f32v(o_t1, 512)
        xbf = lambda i: bf16v(o_xbf + i * 1024, 512)
        x2bf = lambda i: bf16v(o_x2bf + i * 1024, 512)
        wb = lambda i: bf16v(o_wbuf + i * 2048, 1024)
        wpb = lambda i: bf16v(o_wpbuf + i * 8192, 4096)
        lnt1 = f32v(o_lnt1, 512)

        o_w1 = o_h
        o_w2 = o_w1 + 8192 * 2
        o_ln5 = o_w2 + 8192 * 2
        o_at1 = o_ln5 + KC * TW * 2
        o_at2 = o_at1 + 512 * 4
        o_at3 = o_at2 + 512 * 4
        o_ebc = o_at3 + 512 * 4
        o_kvf = o_ebc + 1024 * 4
        o_kvbf = o_kvf + 520 * 4
        o_zd = o_kvbf + 520 * 2 + 4
        o_zrow = o_zd + 128 * 2
        o_swp = o_zrow + 512 * 4
        o_kf = o_swp
        o_vext = o_kf + 1024 * 2
        o_bkr = o_vext + 1040 * 2
        o_bvr = o_bkr + D * 2
        o_qf = o_swp
        o_ysb = o_qf
        o_zbc = o_swp + KC * TW * 2
        o_kvbd = o_zbc + 512 * 4
        attn_end = o_kvbd + KC * 128 * 2

        w1 = lambda w0, n: bf16v(o_w1 + w0 * 2, n)
        w2 = lambda w0, n: bf16v(o_w2 + w0 * 2, n)
        ln5 = lambda c, j0, n: bf16v(o_ln5 + (c * TW + j0) * 2, n)
        at1 = f32v(o_at1, 512)
        at2 = f32v(o_at2, 512)
        at3 = f32v(o_at3, 512)
        axbf = lambda i: bf16v(o_at1 + i * 1024, 512)
        ax2bf = lambda i: bf16v(o_at2 + i * 1024, 512)
        ebc = f32v(o_ebc, KC * 128, 0, H)
        kv_f = f32v(o_kvf, 520)
        sx2b = lambda i: bf16v(o_kvf + i * 1024, 512)
        kv_bf = bf16v(o_kvbf, 520)
        zdiag = bf16v(o_zd, 128)
        zrow = f32v(o_zrow, 512, 0, H)
        kf_sb = lambda h0, n: bf16v(o_kf + h0 * 2, n)
        vext = bf16v(o_vext, 1040)
        bk_row = bf16v(o_bkr, D, 0, 1)
        bv_row = bf16v(o_bvr, D, 0, 1)
        qf_sb = lambda c: bf16v(o_qf + c * TW * 2, TW)
        y_sb = lambda c: bf16v(o_ysb + c * TW * 2, TW)
        zbc_sb = f32v(o_zbc, 512)
        sxb = lambda i: bf16v(o_zbc + i * 1024, 512)
        kvbd = bf16v(o_kvbd, KC * 128)

        assert max(pro_end, main_end, attn_end) <= words * 4, (
            pro_end, main_end, attn_end, words * 4)

        def dma(eng, sem, out, in_, waits=(), serial=False):
            if serial:
                waits = list(waits) + [(sem, sem.n)]
            P.op(eng, lambda e, o=out, i=in_: e.dma_start(out=o, in_=i),
                 waits=waits, inc=sem, inc_k=16, chain=False)

        dma("sync", sPRO, eye, eye_in, serial=True)
        dma("gpsimd", gW, afc_bf, afc_in)
        dma("gpsimd", gW, apj_bf, apj_in)
        ab_loaded = gW.n
        dma("sync", sPRO, b2st_fc, bfc2_in, serial=True)
        w_ = sPRO.n
        P.op("vector", lambda e: e.tensor_scalar(b2fc, b2st_fc, 2.0, None, ALU.mult),
             waits=[(sPRO, w_)], inc=sV)
        b2fc_v = sV.n
        dma("sync", sPRO, b2st_pj, bpj2_in, waits=[(sV, b2fc_v)], serial=True)
        w_ = sPRO.n
        P.op("vector", lambda e: e.tensor_scalar(b2pj, b2st_pj, 1.0, None, ALU.mult),
             waits=[(sPRO, w_)], inc=sV)
        b2_v = sV.n
        P.op("vector", lambda e: e.memset(ones_cbf, 1.0), inc=sV)
        P.op("vector", lambda e: e.memset(ones_rbf, 1.0), inc=sV)
        P.op("vector", lambda e: e.memset(ones_rf, 1.0), inc=sV)
        P.op("vector", lambda e: e.memset(half_col, 0.5), inc=sV)
        P.op("vector", lambda e: e.memset(zero_col, 0.0), inc=sV)

        def bias_cols(src_dram, nchunks, dst_col):
            t = f32v(o_btmp, 128, 0, nchunks)
            dma("sync", sPRO, t, src_dram, waits=[(sV, sV.n), (sT, sT.n)], serial=True)
            wd = sPRO.n
            P.op("tensor", lambda e, t=t, n=nchunks: e.transpose(
                pb[0][:, 0:n], t, eye[0:n, 0:n]),
                waits=[(sPRO, wd), (sV, sV.n)], inc=sT)
            P.op("vector", lambda e, d=dst_col, n=nchunks: e.tensor_copy(
                d, pb[0][:, 0:n]), waits=[(sT, sT.n)], inc=sV)

        bias_cols(bfc_in, FC, bfc_col)
        bias_cols(bpj_in, KC, bpj_col)
        bias_cols(bo_in, KC, bo_col)
        bias_cols(bq_in, KC, bq_col)
        P.op("vector", lambda e: e.tensor_scalar(bpjh_col, bpj_col, 0.5, None, ALU.mult),
             inc=sV)

        ba_ev = {}
        uslot = [0]
        qstate = {"uidx": 0, "evac": [0, 0], "act2": [0, 0], "pend": [None]}

        def quant_weight(w_dram, nchunks, chunk_w, a_bf, b2_bf, scr_ap, nout,
                         prev_scr, sc_fac=1.0):
            grp_ch = 8192 // chunk_w
            ngrp = nchunks // grp_ch
            kj = nchunks * 128

            red_at = {}
            for g in range(ngrp):
                h = g % 2
                lane = sSTGl[h]
                for ci in range(grp_ch):
                    k = g * grp_ch + ci
                    w = []
                    if ci < 2:
                        w = ([(sV, red_at[g - 2])] if g - 2 in red_at
                             else [(sV, sV.n), (sG, sG.n), (sS, sS.n)])
                    dma("sync" if ci % 2 == 0 else "scalar", lane,
                        stage(ci * chunk_w, chunk_w, h),
                        w_dram[k * 128:(k + 1) * 128, :], waits=w)
                wd = lane.n
                if g == 0:
                    P.op("vector", lambda e, h=h: e.reduce_max(
                        amax_col, stage(0, 8192, h), AX.X, apply_absolute_value=True),
                        waits=[(lane, wd)], inc=sV)
                else:
                    P.op("vector", lambda e, h=h: e.reduce_max(
                        f32v(o_amaxc + 4, 1), stage(0, 8192, h), AX.X,
                        apply_absolute_value=True), waits=[(lane, wd)], inc=sV)
                    P.op("vector", lambda e: e.tensor_tensor(
                        amax_col, amax_col, f32v(o_amaxc + 4, 1), ALU.max), inc=sV)
                red_at[g] = sV.n
            wv = sV.n
            P.op("tensor", lambda e: e.transpose(pb[0][0:1, 0:128], amax_col, eye),
                 waits=[(sV, wv), (sPRO, sPRO.n)], inc=sT)
            P.op("vector", lambda e: e.reduce_max(
                srow[0:1, 0:1], pb[0][0:1, 0:128], AX.X, apply_absolute_value=True),
                waits=[(sT, sT.n)], inc=sV)
            P.op("vector", lambda e: e.memset(srow[0:1, 4:5], 127.0), inc=sV)
            P.op("vector", lambda e: e.reciprocal(srow[0:1, 5:6], srow[0:1, 4:5]), inc=sV)
            P.op("vector", lambda e: e.tensor_tensor(
                srow[0:1, 1:2], srow[0:1, 0:1], srow[0:1, 5:6], ALU.mult), inc=sV)
            P.op("vector", lambda e: e.reciprocal(srow[0:1, 2:3], srow[0:1, 1:2]), inc=sV)
            wv = sV.n
            P.op("tensor", lambda e: e.matmul(
                pb[0][:, 0:2], ones_rf, srow[0:1, 1:3], start=True, stop=True,
                skip_group_check=True), waits=[(sV, wv)], inc=sT)
            P.op("vector", lambda e: e.tensor_scalar(
                sc_col, pb[0][:, 0:1], sc_fac, None, ALU.mult),
                waits=[(sT, sT.n)], inc=sV)
            P.op("vector", lambda e: e.tensor_copy(inv_col, pb[0][:, 1:2]), inc=sV)
            scale_v = sV.n

            ba_done = {}
            nslice = nchunks * chunk_w // 512
            for i in range(nslice):
                k = (i * 512) // chunk_w
                col0 = (i * 512) % chunk_w
                bank = pb[2 + i % 4]
                w = [(gW, ab_loaded), (sV, b2_v)]
                if i - 4 in ba_done:
                    w.append((sS, ba_done[i - 4]))
                if i == 0 and prev_scr:
                    w.append((sSCW, prev_scr))
                P.op("tensor", lambda e, k=k, col0=col0, bank=bank, a_bf=a_bf,
                     b2_bf=b2_bf: e.matmul(
                        bank[:, :], a_bf[:, k * 128:(k + 1) * 128],
                        b2_bf[:, col0:col0 + 512], start=True, stop=True,
                        skip_group_check=True), waits=w, inc=sT)
                wt = sT.n
                base_o = col0 // 128
                dst = weff_flat.rearrange("p (o kj) -> p o kj", kj=kj)[
                    :, base_o:base_o + 4, k * 128:(k + 1) * 128]
                sw = [(sT, wt)]
                if i == 0 and prev_scr:
                    sw.append((sSCW, prev_scr))
                P.op("scalar", lambda e, dst=dst, bank=bank: e.copy(
                    dst, bank[:, :].rearrange("p (o j) -> p o j", j=128)),
                    waits=sw, inc=sS)
                ba_done[i] = sS.n
            ba_s = sS.n

            units_per_grp = 8192 // 2048
            unit_at = {}
            for g in range(ngrp):
                h = g % 2
                lane = sSTGl[h]
                for ci in range(grp_ch):
                    k = g * grp_ch + ci
                    w = []
                    if ci == 0:
                        w = ([(sS, unit_at[g - 2])] if g - 2 in unit_at
                             else [(sV, sV.n), (sS, sS.n)])
                    dma("sync", lane, stage(ci * chunk_w, chunk_w, h),
                        w_dram[k * 128:(k + 1) * 128, :], waits=w)
                wd = lane.n
                for u in range(units_per_grp):
                    si = qstate["uidx"] % 2
                    qt, qq, qc, qi, o_qtX = qset[si]
                    st = stage(u * 2048, 2048, h)
                    sw = [(lane, wd), (sV, scale_v)]
                    if qstate["evac"][si]:
                        sw.append((sG, qstate["evac"][si]))
                    P.op("scalar", lambda e, st=st, qt=qt: e.activation(
                        qt, st, ACT.Identity, bias=half_col, scale=inv_col),
                        waits=sw, inc=sS)
                    s1_cnt = sS.n
                    if qstate["pend"][0] is not None:
                        qstate["pend"][0]()
                        qstate["pend"][0] = None
                    vw = [(sS, s1_cnt)]
                    if qstate["act2"][si]:
                        vw.append((sS, qstate["act2"][si]))
                    P.op("vector", lambda e, qi=qi, qt=qt: e.tensor_copy(qi, qt),
                         waits=vw)
                    P.op("vector", lambda e, qq=qq, qi=qi: e.tensor_copy(qq, qi))
                    P.op("vector", lambda e, qc=qc, qq=qq, qt=qt: e.tensor_tensor(
                        qc, qq, qt, ALU.is_gt))
                    P.op("vector", lambda e, qq=qq, qc=qc: e.tensor_tensor(
                        qq, qq, qc, ALU.subtract))
                    sub_cnt = sV.n

                    pieces = []
                    for piece in range(2):
                        gcol = g * 8192 + u * 2048 + piece * 1024
                        kk = gcol // chunk_w
                        s = (gcol % chunk_w) // 1024
                        base_o = (s * 1024) // 128
                        dst = weff_flat.rearrange("p (o kj) -> p o kj", kj=kj)[
                            :, base_o:base_o + 8, kk * 128:(kk + 1) * 128]
                        qsrc = f32v(o_qtX + piece * 1024 * 4, 1024).rearrange(
                            "p (o j) -> p o j", j=128)
                        pieces.append((dst, qsrc))

                    def mk_tail(qt=qt, qq=qq, sub_cnt=sub_cnt, si=si,
                                pieces=pieces):
                        def emit():
                            P.op("scalar", lambda e, qt=qt, qq=qq: e.activation(
                                qt, qq, ACT.Identity, bias=zero_col, scale=sc_col),
                                waits=[(sV, sub_cnt)], inc=sS)
                            a2 = sS.n
                            qstate["act2"][si] = a2
                            for j, (dst, qsrc) in enumerate(pieces):
                                P.op("gpsimd", lambda e, dst=dst, qsrc=qsrc:
                                     e.tensor_tensor(dst, dst, qsrc, ALU.add),
                                     waits=[(sS, a2)] if j == 0 else ())
                            qstate["evac"][si] = sG.n
                        return emit

                    qstate["pend"][0] = mk_tail()
                    qstate["uidx"] += 1
                unit_at[g] = sS.n
            if qstate["pend"][0] is not None:
                qstate["pend"][0]()
                qstate["pend"][0] = None
            wv = sV.n
            wg = sG.n
            for o in range(nout):
                src_ = bf16v(o_weff + o * kj * 2, kj)
                dma("sync", sSCW, scr_ap[o], src_,
                    waits=[(sV, wv), (sG, wg), (sS, sS.n)] if o == 0 else ())

        quant_weight(wfc_t, KC, F, afc_bf, b2fc, scr_fc, FC, 0)
        fc_scr_done = sSCW.n
        quant_weight(wpj_t, FC, D, apj_bf, b2pj, scr_pj, KC, fc_scr_done,
                     sc_fac=0.5)
        scr_all = sSCW.n

        for k in range(KC):
            if k >= 4:
                w = [(sSCW, scr_all)]
            else:
                w = [(sV, sV.n), (sG, sG.n), (sS, sS.n)]
            dma("sync", sXl[k], Xv(k, 0, T), x_in[k * 128:(k + 1) * 128, :], waits=w)

        def ln_stats(t0, xwaits_fn, xb, x2b, bankS=None, bankQ=None,
                     prowS=0, prowQ=0, cast_eng="vector"):
            bankS = pb[0] if bankS is None else bankS
            bankQ = pb[1] if bankQ is None else bankQ
            tmm = {}
            smm = {}
            t_at_call = sT.n
            s_at_call = sS.n
            for c in range(KC):
                i = c % 2
                w = list(xwaits_fn(c))
                if c - 2 in tmm:
                    w += [(sT, tmm[c - 2]), (sS, smm[c - 2])]
                else:
                    w += [(sT, t_at_call), (sS, s_at_call)]
                if cast_eng == "vector":
                    P.op("vector", lambda e, c=c, i=i, t0=t0: e.tensor_copy(
                        xb(i), Xv(c, t0, TW)), waits=w, inc=sV)
                    cast_w = (sV, sV.n)
                else:
                    P.op("scalar", lambda e, c=c, i=i, t0=t0: e.activation(
                        xb(i), Xv(c, t0, TW), ACT.Identity), waits=w, inc=sS)
                    cast_w = (sS, sS.n)
                wv = cast_w[1]
                sqw = [cast_w]
                if c - 2 in tmm:
                    sqw.append((sT, tmm[c - 2]))
                P.op("scalar", lambda e, i=i: e.activation(x2b(i), xb(i), ACT.Square),
                     waits=sqw, inc=sS)
                smm[c] = sS.n
                P.op("tensor", lambda e, c=c, i=i: e.matmul(
                    bankS[prowS:prowS + 1, :], ones_cbf, xb(i),
                    start=(c == 0), stop=(c == KC - 1),
                    tile_position=(0, prowS), skip_group_check=True),
                    waits=[cast_w], inc=None)
                P.op("tensor", lambda e, c=c, i=i: e.matmul(
                    bankQ[prowQ:prowQ + 1, :], ones_cbf, x2b(i),
                    start=(c == 0 and bankQ is not bankS), stop=(c == KC - 1),
                    tile_position=(0, prowQ), skip_group_check=True),
                    waits=[(sS, smm[c])], inc=sT)
                tmm[c] = sT.n
            wt = tmm[KC - 1]
            P.op("vector", lambda e: e.tensor_scalar(
                rowm, bankS[prowS:prowS + 1, :], 1.0 / D, None, ALU.mult),
                waits=[(sT, wt)], inc=sV)
            P.op("vector", lambda e: e.tensor_scalar(
                rowv, bankQ[prowQ:prowQ + 1, :], 1.0 / D, LN_EPS, ALU.mult, ALU.add),
                inc=sV)
            P.op("vector", lambda e: e.tensor_tensor(rowb, rowm, rowm, ALU.mult),
                 waits=[(sRST, sRST.n)], inc=sV)
            P.op("vector", lambda e: e.tensor_tensor(rowv, rowv, rowb, ALU.subtract), inc=sV)
            P.op("vector", lambda e: e.reciprocal(rowb, rowv), inc=sV)
            wv = sV.n
            P.op("scalar", lambda e: e.activation(rowa, rowb, ACT.Sqrt),
                 waits=[(sV, wv), (sRST, sRST.n)], inc=sS)
            ws = sS.n
            P.op("vector", lambda e: e.tensor_tensor(rowb, rowm, rowa, ALU.mult),
                 waits=[(sS, ws)], inc=sV)
            return sV.n

        def ln_bcast(rows_v, a_src, b_src, bankA, bankB, extra=()):
            w = [(sV, rows_v)] + list(extra)
            P.op("tensor", lambda e, a=a_src, bankA=bankA: e.matmul(
                bankA[:, :], ones_rf, a, start=True, stop=True,
                skip_group_check=True), waits=w, inc=sT)
            P.op("tensor", lambda e, b=b_src, bankB=bankB: e.matmul(
                bankB[:, :], ones_rf, b, start=True, stop=True,
                skip_group_check=True), inc=sT)
            return sT.n

        def ln_apply(t0, bc_t, bankA, bankB, dst_fn, scratch=None, dst_free_t=0):
            sc = t1v if scratch is None else scratch
            for c in range(KC):
                P.op("vector", lambda e, c=c, t0=t0, bankA=bankA, sc=sc: e.tensor_tensor(
                    sc, Xv(c, t0, TW), bankA[:, :], ALU.mult),
                    waits=[(sT, max(bc_t, dst_free_t))], inc=sV)
                P.op("vector", lambda e, c=c, bankB=bankB, d=dst_fn(c), sc=sc: e.tensor_tensor(
                    d, sc, bankB[:, :], ALU.subtract), inc=sV)
            return sV.n

        def ln_apply_chunks(t0, bc_t, bankA, bankB, dst_fn, dst_free_t=0):
            ats = []
            for c in range(KC):
                P.op("vector", lambda e, c=c, t0=t0, bankA=bankA: e.tensor_tensor(
                    t1v, Xv(c, t0, TW), bankA[:, :], ALU.mult),
                    waits=[(sT, max(bc_t, dst_free_t))], inc=sV)
                P.op("vector", lambda e, c=c, bankB=bankB, d=dst_fn(c): e.tensor_tensor(
                    d, t1v, bankB[:, :], ALU.subtract), inc=sV)
                ats.append(sV.n)
            return ats

        fcg = [0]; pjg = [0]
        fcmm_at = {}; fcev_at = {}; pjmm_at = {}; pjev_at = {}

        def ffn_pass(pass_idx):
            last_rv2 = [0]

            def stats_or_rows(tile, lnr_v=0):
                t0 = tile * TW
                if pass_idx == 0:
                    return ln_stats(t0, lambda c: [(sXl[c], 16)], xbf, x2bf)
                dma("sync", sROW, rowa, ln3_dram[0:1, t0:t0 + TW],
                    waits=[(sV, sV.n), (sT, sT.n), (sS, sS.n), (sRST, sRST.n)]
                    + ([(dOUT, dOUT.n)] if debug else []))
                dma("sync", sROW, rowb, ln3_dram[1:2, t0:t0 + TW])
                return 0

            def bcast(rv):
                extra = [(sROW, sROW.n)] if pass_idx == 1 else []
                return ln_bcast(rv, rowa, rowb, pb[2], pb[3], extra=extra)

            rv = stats_or_rows(0)
            bc = ln_bcast(rv, rowa, rowb, pb[2], pb[3],
                          extra=([(sROW, sROW.n), (sV, sV.n)] if pass_idx == 1 else []))
            lnr = ln_apply(0, bc, pb[2], pb[3], lambda c: lnv(c, 0, TW),
                           scratch=lnt1, dst_free_t=sT.n if pass_idx == 1 else 0)

            for tile in range(NT):
                t0 = tile * TW
                if tile + 1 < NT:
                    rv_n = stats_or_rows(tile + 1, lnr)
                for o in range(FC):
                    g = fcg[0]
                    buf = wb(g % 3)
                    lane = (dWa, dWb, dWc)[g % 3]
                    w = [(sSCW, scr_all)]
                    if g - 3 in fcmm_at:
                        w.append((sT, fcmm_at[g - 3]))
                    dma("sync", lane, buf, scr_fc[o], waits=w)
                    wd = lane.n
                    bank = pb[4 + (o % 2)]
                    w0 = [(lane, wd), (sV, lnr)]
                    if g - 2 in fcev_at:
                        w0.append((sS, fcev_at[g - 2]))
                    for k in range(KC):
                        P.op("tensor", lambda e, k=k, buf=buf, bank=bank: e.matmul(
                            bank[:, :], buf[:, k * 128:(k + 1) * 128], lnv(k, 0, TW),
                            start=(k == 0), stop=(k == KC - 1), skip_group_check=True),
                            waits=w0 if k == 0 else (), inc=sT if k == KC - 1 else None)
                    fcmm_at[g] = sT.n
                    P.op("scalar", lambda e, o=o, bank=bank: e.activation(
                        hv(o), bank[:, :], ACT.Gelu, bias=bfc_col[:, o:o + 1], scale=1.0),
                        waits=[(sT, fcmm_at[g])], inc=sS)
                    fcev_at[g] = sS.n
                    fcg[0] += 1
                    if o == 16 and tile + 1 < NT:
                        bc_n = bcast(rv_n)
                        fc_last_future = sT.n + (FC - 1 - o)
                        lnr_next = ln_apply((tile + 1) * TW, bc_n, pb[2], pb[3],
                                            lambda c: lnv(c, 0, TW), scratch=lnt1,
                                            dst_free_t=fc_last_future)

                for o in range(KC):
                    g = pjg[0]
                    lane = dPa if g % 2 == 0 else dPb
                    w = [(sSCW, scr_all)]
                    if g - 2 in pjmm_at:
                        w.append((sT, pjmm_at[g - 2]))
                    buf = wpb(g % 2)
                    dma("sync", lane, buf, scr_pj[o], waits=w)
                    wd = lane.n
                    bank = pb[6 + (o % 2)]
                    w0 = [(lane, wd), (sS, fcev_at[fcg[0] - 1])]
                    if g - 2 in pjev_at:
                        w0.append((sV, pjev_at[g - 2]))
                    for k in range(FC):
                        P.op("tensor", lambda e, k=k, buf=buf, bank=bank: e.matmul(
                            bank[:, :], buf[:, k * 128:(k + 1) * 128], hv(k),
                            start=(k == 0), stop=(k == FC - 1), skip_group_check=True),
                            waits=w0 if k == 0 else (), inc=sT if k == FC - 1 else None)
                    pjmm_at[g] = sT.n
                    P.op("vector", lambda e, o=o, t0=t0, bank=bank:
                         e.scalar_tensor_tensor(
                            Xv(o, t0, TW), bank[:, :], bpjh_col[:, o:o + 1],
                            Xv(o, t0, TW), ALU.add, ALU.add),
                        waits=[(sT, pjmm_at[g])], inc=sV)
                    pjev_at[g] = sV.n
                    if pass_idx == 1:
                        dma("gpsimd", gOUT, out_x[o * 128:(o + 1) * 128, t0:t0 + TW],
                            Xv(o, t0, TW), waits=[(sV, pjev_at[g])])
                    pjg[0] += 1

                if pass_idx == 0:
                    gbase = pjg[0] - KC
                    rv2 = ln_stats(t0, lambda c: [(sV, pjev_at[gbase + c])],
                                   xbf, x2bf)
                    last_rv2[0] = rv2
                    dma("sync", sRST, ln2_dram[0:1, t0:t0 + TW], rowa, waits=[(sV, rv2)])
                    dma("sync", sRST, ln2_dram[1:2, t0:t0 + TW], rowb)
                if tile + 1 < NT:
                    lnr = lnr_next

        ffn_pass(0)
        ffn1_v = sV.n
        ffn1_mm_t = pjmm_at[pjg[0] - 1]
        ffn1_ge_s = fcev_at[fcg[0] - 1]
        if debug:
            for k in range(KC):
                dma("sync", dOUT, dbg_x1[k * 128:(k + 1) * 128, :], Xv(k, 0, T),
                    waits=[(sV, ffn1_v)])
            dma("sync", dOUT, dbg_ln2, ln2_dram, waits=[(sRST, sRST.n)])
            for o in range(FC):
                dma("sync", dOUT, dbg_wfc[o], scr_fc[o])
            for o in range(KC):
                dma("sync", dOUT, dbg_wpj[o], scr_pj[o])
        ffn1_rst = sRST.n

        dma("sync", sPRO, ebc, ebc_in, waits=[(sT, ffn1_mm_t)], serial=True)
        ebc_d = sPRO.n
        dma("gpsimd", gW, bk_row, bk_in, waits=[(sT, ffn1_mm_t), (sS, ffn1_ge_s)])
        dma("gpsimd", gW, bv_row, bv_in)
        for k in range(KC):
            P.op("gpsimd", lambda e, k=k: e.dma_start(
                out=w1(k * 1024, 1024), in_=wk_in[k * 128:(k + 1) * 128, :]),
                waits=[(sT, ffn1_mm_t), (sS, ffn1_ge_s)], inc=gW, inc_k=16, chain=False)
            P.op("gpsimd", lambda e, k=k: e.dma_start(
                out=w2(k * 1024, 1024), in_=wv_in[k * 128:(k + 1) * 128, :]),
                inc=gW, inc_k=16, chain=False)
        wkv_d = gW.n
        P.op("vector", lambda e: e.memset(
            vext.rearrange("p (h m) -> p h m", m=65)[:, :, 64:65], 1.0),
            waits=[(sT, ffn1_mm_t)], inc=sV)
        vones_v = sV.n

        kv_started = [False, False]
        prev_kv_t = [0]
        prev_ex_s = [0]
        pend_kv = [None]
        ev_free = [0, 0, 0, 0]
        last_kmm = [0]
        last_vmm = [0]

        dma("sync", sROW, rowa, ln2_dram[0:1, 0:TW],
            waits=[(sRST, ffn1_rst), (sT, sT.n), (sV, sV.n), (sS, sS.n),
                   (dOUT, dOUT.n)])
        dma("sync", sROW, rowb, ln2_dram[1:2, 0:TW])
        bc = ln_bcast(0, rowa, rowb, pb[0], pb[1],
                      extra=[(sROW, sROW.n), (sV, vones_v), (sPRO, ebc_d)])
        apply_at = ln_apply_chunks(0, bc, pb[0], pb[1], lambda c: ln5(c, 0, TW))
        bc_t_cur = bc
        for g5 in range(NT):
            t0 = g5 * TW
            for mt in range(4):
                j0 = mt * 128
                is_last = (g5 == NT - 1 and mt == 3)
                for half in range(2):
                    for k in range(KC):
                        w = []
                        if mt == 0:
                            w.append((sV, apply_at[k]))
                        if k == 0:
                            w += [(gW, wkv_d), (sV, ev_free[half])]
                        P.op("tensor", lambda e, k=k, half=half, j0=j0: e.matmul(
                            pb[4 + half][:, :], ln5(k, j0, 128),
                            w1(k * 1024 + half * 512, 512),
                            start=(k == 0), stop=False, skip_group_check=True),
                            waits=w, inc=None)
                    P.op("tensor", lambda e, half=half: e.matmul(
                        pb[4 + half][:, :], ones_rbf,
                        bk_row[0:1, half * 512:half * 512 + 512],
                        start=False, stop=True, skip_group_check=True), inc=sT)
                kmm = sT.n
                last_kmm[0] = kmm
                for half in range(2):
                    for k in range(KC):
                        w = []
                        if mt == 0:
                            w.append((sV, apply_at[k]))
                        if k == 0:
                            w.append((sV, ev_free[2 + half]))
                        P.op("tensor", lambda e, k=k, half=half, j0=j0: e.matmul(
                            pb[6 + half][:, :], ln5(k, j0, 128),
                            w2(k * 1024 + half * 512, 512),
                            start=(k == 0), stop=False, skip_group_check=True),
                            waits=w, inc=None)
                    P.op("tensor", lambda e, half=half: e.matmul(
                        pb[6 + half][:, :], ones_rbf,
                        bv_row[0:1, half * 512:half * 512 + 512],
                        start=False, stop=True, skip_group_check=True), inc=sT)
                vmm = sT.n
                last_vmm[0] = vmm
                if pend_kv[0] is not None:
                    pend_kv[0]()
                    pend_kv[0] = None
                for half in range(2):
                    P.op("vector", lambda e, half=half: e.tensor_scalar(
                        at1, pb[4 + half][:, :], 0.0, None, ALU.min),
                        waits=[(sT, kmm), (sS, prev_ex_s[0])], inc=sV)
                    wv = sV.n
                    P.op("scalar", lambda e: e.activation(at2, at1, ACT.Exp),
                         waits=[(sV, wv)], inc=sS)
                    prev_ex_s[0] = sS.n
                    P.op("vector", lambda e, half=half: e.tensor_scalar(
                        at1, pb[4 + half][:, :], 0.0, None, ALU.max),
                        waits=[(sS, prev_ex_s[0])], inc=sV)
                    ev_free[half] = sV.n
                    P.op("vector", lambda e, half=half: e.tensor_tensor(
                        kf_sb(half * 512, 512), at2, at1, ALU.add),
                        waits=[(sT, prev_kv_t[0])], inc=sV)
                    dsth = vext.rearrange("p (h m) -> p h m", m=65)[
                        :, half * 8:(half + 1) * 8, 0:64]
                    srch = pb[6 + half][:, :].rearrange("p (h m) -> p h m", m=64)
                    P.op("vector", lambda e, dsth=dsth, srch=srch: e.tensor_copy(
                        dsth, srch), waits=[(sT, vmm)], inc=sV)
                    ev_free[2 + half] = sV.n
                kvr = sV.n

                def mk_kv(kvr=kvr, is_last=is_last):
                    def emit():
                        for hh in range(H):
                            c = hh // 2
                            bank = pb[2 + (c // 4)]
                            col0 = (c % 4) * 65
                            po = 64 * (hh % 2)
                            fb = not kv_started[c // 4]
                            kv_started[c // 4] = True
                            P.op("tensor", lambda e, hh=hh, bank=bank, col0=col0,
                                 po=po, fb=fb, last=is_last: e.matmul(
                                    bank[po:po + 64, col0:col0 + 65],
                                    kf_sb(hh * 64, 64), vext[:, hh * 65:(hh + 1) * 65],
                                    start=fb, stop=(last and hh >= H - 2),
                                    tile_position=(0, po), skip_group_check=True),
                                waits=[(sV, kvr)] if hh == 0 else (),
                                inc=sT if hh == H - 1 else None)
                        prev_kv_t[0] = sT.n
                    return emit

                pend_kv[0] = mk_kv()
            if g5 + 1 < NT:
                t1n = (g5 + 1) * TW
                dma("sync", sROW, rowa, ln2_dram[0:1, t1n:t1n + TW],
                    waits=[(sT, bc_t_cur)])
                dma("sync", sROW, rowb, ln2_dram[1:2, t1n:t1n + TW])
                bc = ln_bcast(apply_at[KC - 1], rowa, rowb, pb[0], pb[1],
                              extra=[(sROW, sROW.n)])
                bc_t_cur = bc
                pend_kv[0]()
                pend_kv[0] = None
                apply_at = ln_apply_chunks(t1n, bc, pb[0], pb[1],
                                           lambda c: ln5(c, 0, TW),
                                           dst_free_t=last_vmm[0])
        pend_kv[0]()
        pend_kv[0] = None
        kv_done_t = sT.n
        s1_lnr = sV.n

        for k in range(KC):
            P.op("gpsimd", lambda e, k=k: e.dma_start(
                out=w1(k * 1024, 1024), in_=wq_in[k * 128:(k + 1) * 128, :]),
                waits=[(sT, last_kmm[0])] if k == 0 else (),
                inc=gW, inc_k=16, chain=False)
        for k in range(KC):
            P.op("gpsimd", lambda e, k=k: e.dma_start(
                out=w2(k * 1024, 1024), in_=wo_in[k * 128:(k + 1) * 128, :]),
                waits=[(sT, last_vmm[0])] if k == 0 else (),
                inc=gW, inc_k=16, chain=False)
        wqo_d = gW.n

        P.op("vector", lambda e: e.tensor_copy(kv_f[:, 0:260], pb[2][:, 0:260]),
             waits=[(sT, kv_done_t)], inc=sV)
        P.op("vector", lambda e: e.tensor_copy(kv_f[:, 260:520], pb[3][:, 0:260]), inc=sV)
        kvevac_v = sV.n
        wv = sV.n
        P.op("gpsimd", lambda e: e.dma_start(out=cc_in.ap(), in_=kv_f),
             waits=[(sV, wv)], inc=gCC, inc_k=16, chain=False)
        P.op("gpsimd", lambda e: e.collective_compute(
            "AllReduce", ALU.add, replica_groups=[[0, 1], [2, 3], [4, 5], [6, 7]],
            ins=[cc_in.ap()], outs=[cc_out.ap()]),
            waits=[(gCC, gCC.n)], inc=sCC, chain=False)
        P.op("gpsimd", lambda e: e.dma_start(out=kv_f, in_=cc_out.ap()),
             waits=[(sCC, sCC.n)], inc=gCC, inc_k=16, chain=False)
        wg = gCC.n
        kvr_box = [0]

        def emit_kv_post():
            P.op("vector", lambda e: e.tensor_copy(kv_bf, kv_f), waits=[(gCC, wg)], inc=sV)
            P.op("vector", lambda e: e.memset(zdiag, 0.0), inc=sV)
            for c in range(KC):
                kcol = 260 * (c // 4) + (c % 4) * 65 + 64
                P.op("vector", lambda e, c=c, kcol=kcol: e.tensor_copy(
                    zdiag[0:64, c * 16 + 2 * c:c * 16 + 2 * c + 1],
                    kv_bf[0:64, kcol:kcol + 1]), inc=sV)
                P.op("vector", lambda e, c=c, kcol=kcol: e.tensor_copy(
                    zdiag[64:128, c * 16 + 2 * c + 1:c * 16 + 2 * c + 2],
                    kv_bf[64:128, kcol:kcol + 1]), inc=sV)
            P.op("vector", lambda e: e.memset(kvbd, 0.0), inc=sV)
            for c in range(KC):
                kcol = 260 * (c // 4) + (c % 4) * 65
                P.op("vector", lambda e, c=c, kcol=kcol: e.tensor_copy(
                    kvbd[0:64, c * 128:c * 128 + 64], kv_bf[0:64, kcol:kcol + 64]), inc=sV)
                P.op("vector", lambda e, c=c, kcol=kcol: e.tensor_copy(
                    kvbd[64:128, c * 128 + 64:(c + 1) * 128],
                    kv_bf[64:128, kcol:kcol + 64]), inc=sV)
            kvr_box[0] = sV.n


        dma("sync", sROW, rowa, ln2_dram[0:1, 0:TW],
            waits=[(sT, sT.n), (sV, s1_lnr), (sS, sS.n), (sRST, sRST.n)]
            + ([(dOUT, dOUT.n)] if debug else []))
        dma("sync", sROW, rowb, ln2_dram[1:2, 0:TW])
        bc = ln_bcast(0, rowa, rowb, pb[0], pb[1],
                      extra=[(sROW, sROW.n), (sV, s1_lnr)])
        lnr = ln_apply(0, bc, pb[0], pb[1], lambda c: ln5(c, 0, TW),
                       dst_free_t=kv_done_t)
        bc_cur = bc
        ysc_at = {}
        qact_c = {}

        def emit_q(lnr_v, oproj_gate_t, pb01_free_v=0):
            qv_at = {}
            qs_at = {}
            qact_c.clear()
            for c in range(KC):
                bank = pb[c % 2]
                w0 = [(gW, wqo_d), (sV, lnr_v)]
                if c == 0 and pb01_free_v:
                    w0.append((sV, pb01_free_v))
                if c - 2 in qact_c:
                    w0.append((sS, qact_c[c - 2]))
                for k in range(KC):
                    P.op("tensor", lambda e, c=c, k=k, bank=bank: e.matmul(
                        bank[:, :], w1(k * 1024 + c * 128, 128), ln5(k, 0, TW),
                        start=(k == 0), stop=(k == KC - 1), skip_group_check=True),
                        waits=w0 if k == 0 else (), inc=sT if k == KC - 1 else None)
                qmm = sT.n
                cw = [(sT, qmm)]
                if c - 1 in qv_at:
                    cw.append((sV, qv_at[c - 1]))
                P.op("scalar", lambda e, c=c, bank=bank: e.activation(
                    at1, bank[:, :], ACT.Identity, bias=bq_col[:, c:c + 1], scale=1.0),
                    waits=cw, inc=sS)
                qact_c[c] = sS.n
                P.op("vector", lambda e: e.tensor_scalar(at2, at1, 0.0, None, ALU.min),
                     waits=[(sS, qact_c[c])], inc=sV)
                wv = sV.n
                P.op("scalar", lambda e: e.activation(at3, at2, ACT.Exp),
                     waits=[(sV, wv)], inc=sS)
                qs_at[c] = sS.n
                P.op("vector", lambda e: e.tensor_scalar(at1, at1, 0.0, None, ALU.max),
                     waits=[(sS, qs_at[c])], inc=sV)
                aw = [(sT, oproj_gate_t)] if c == 0 else []
                P.op("vector", lambda e, c=c: e.tensor_tensor(
                    qf_sb(c), at3, at1, ALU.add),
                    waits=aw + ([(dOUT, dOUT.n)] if debug else []), inc=sV)
                qv_at[c] = sV.n
            return sT.n, sV.n

        qmm_last, qf_v = emit_q(lnr, kv_done_t)
        emit_kv_post()
        if debug:
            dma("sync", dOUT, dbg_qf, bf16v(o_qf, KC * TW), waits=[(sV, qf_v)])
            dma("sync", dOUT, dbg_ln5, bf16v(o_ln5, KC * TW),
                waits=[(sV, qf_v), (sT, qmm_last)])
        prev_rv = 0
        for tile in range(NT):
            t0 = tile * TW
            kvr_v = kvr_box[0]
            for c in range(KC):
                P.op("tensor", lambda e, c=c: e.matmul(
                    pb[6][0:16, :], zdiag[:, c * 16:(c + 1) * 16], qf_sb(c),
                    start=(c == 0), stop=(c == KC - 1), skip_group_check=True),
                    waits=[(sV, qf_v), (sV, kvr_v)] if c == 0 else (),
                    inc=sT if c == KC - 1 else None)
            zmm = sT.n
            P.op("vector", lambda e: e.tensor_scalar(
                zrow, pb[6][0:16, :], ATT_EPS, None, ALU.add), waits=[(sT, zmm)], inc=sV)
            P.op("vector", lambda e: e.reciprocal(zrow, zrow), inc=sV)
            z_v = sV.n
            if debug and tile == 0:
                dma("sync", dOUT, dbg_z, zrow, waits=[(sV, z_v)])

            if tile + 1 < NT:
                t1 = (tile + 1) * TW
                dma("sync", sROW, rowa, ln2_dram[0:1, t1:t1 + TW],
                    waits=[(sT, bc_cur), (sRST, sRST.n)])
                dma("sync", sROW, rowb, ln2_dram[1:2, t1:t1 + TW])
                bc_n = ln_bcast(0, rowa, rowb, pb[0], pb[1],
                                extra=[(sROW, sROW.n), (sS, qact_c[KC - 1]),
                                       (sV, prev_rv)]
                                + ([(dOUT, dOUT.n)] if debug else []))
                bc_cur = bc_n

            zbc_2 = [zbc_sb, at3]
            zcopy_prev = [0]
            for c in range(KC):
                ybank = pb[2 + (c % 2)]
                yw = [(sV, z_v)]
                if debug and tile == 0 and c == 0:
                    yw.append((dOUT, dOUT.n))
                if c - 2 in ysc_at:
                    yw.append((sV, ysc_at[c - 2]))
                P.op("tensor", lambda e, c=c, ybank=ybank: e.matmul(
                    ybank[:, :], kvbd[:, c * 128:(c + 1) * 128], qf_sb(c),
                    start=True, stop=True, skip_group_check=True),
                    waits=yw, inc=sT)
                ymm = sT.n
                zbw = [(sT, ymm)]
                if zcopy_prev[0]:
                    zbw.append((sV, zcopy_prev[0]))
                elif c == 0:
                    zbw.append((sV, z_v))
                P.op("tensor", lambda e, c=c: e.matmul(
                    pb[7][:, :], ebc[:, c * 128:(c + 1) * 128], zrow,
                    start=True, stop=True, skip_group_check=True), waits=zbw, inc=sT)
                zb = sT.n
                P.op("vector", lambda e, zc=zbc_2[c % 2]: e.tensor_copy(
                    zc, pb[7][:, :]), waits=[(sT, zb)], inc=sV)
                zcopy_prev[0] = sV.n
                P.op("vector", lambda e, c=c, ybank=ybank, zc=zbc_2[c % 2]:
                     e.tensor_tensor(y_sb(c), ybank[:, :], zc, ALU.mult), inc=sV)
                ysc_at[c] = sV.n
            y_v = sV.n
            if debug and tile == 0:
                dma("sync", dOUT, dbg_y, bf16v(o_ysb, KC * TW), waits=[(sV, y_v)])

            if tile + 1 < NT:
                lnr_next = ln_apply((tile + 1) * TW, bc_cur, pb[0], pb[1],
                                    lambda c: ln5(c, 0, TW), dst_free_t=qmm_last)

            ox_at = {}
            for o in range(KC):
                bank = pb[4 + (o % 2)]
                w0 = [(sV, y_v)]
                if o - 2 in ox_at:
                    w0.append((sV, ox_at[o - 2]))
                for k in range(KC):
                    P.op("tensor", lambda e, o=o, k=k, bank=bank: e.matmul(
                        bank[:, :], w2(k * 1024 + o * 128, 128), y_sb(k),
                        start=(k == 0), stop=(k == KC - 1), skip_group_check=True),
                        waits=w0 if k == 0 else (), inc=sT if k == KC - 1 else None)
                omm = sT.n
                P.op("vector", lambda e, o=o, t0=t0, bank=bank: e.scalar_tensor_tensor(
                    Xv(o, t0, TW), bank[:, :], bo_col[:, o:o + 1], Xv(o, t0, TW),
                    ALU.add, ALU.add), waits=[(sT, omm)], inc=sV)
                ox_at[o] = sV.n
            xupd_v = sV.n
            omm_last = omm

            if tile + 1 < NT:
                qmm_last, qf_v = emit_q(lnr_next, omm_last)

            qg = [(sS, qact_c[KC - 2]), (sS, qact_c[KC - 1])]
            rv = ln_stats(t0, lambda c: [(sV, xupd_v)] + (qg if c == 0 else []),
                          axbf, ax2bf)
            prev_rv = rv
            dma("sync", sRST, ln3_dram[0:1, t0:t0 + TW], rowa, waits=[(sV, rv)])
            dma("sync", sRST, ln3_dram[1:2, t0:t0 + TW], rowb)
        attn_v = sV.n
        if debug:
            for k in range(KC):
                dma("sync", dOUT, dbg_x2[k * 128:(k + 1) * 128, :], Xv(k, 0, T),
                    waits=[(sV, attn_v)])
            dma("sync", dOUT, dbg_kv, kv_f, waits=[(sV, kvr_v)])
        attn_rst = sRST.n

        P.op("sync", lambda e: e.nop(),
             waits=[(sV, attn_v), (sRST, attn_rst), (sT, sT.n), (sS, sS.n),
                    (dOUT, dOUT.n)])
        ffn_pass(1)

        P.op("sync", lambda e: e.nop(),
             waits=[(dOUT, dOUT.n), (gOUT, gOUT.n), (sRST, sRST.n)])

        block = ctx.enter_context(nc.Block())

        @block.sync
        def _(e):
            P.replay("sync", e)

        @block.tensor
        def _(e):
            P.replay("tensor", e)

        @block.vector
        def _(e):
            P.replay("vector", e)

        @block.scalar
        def _(e):
            P.replay("scalar", e)

        @block.gpsimd
        def _(e):
            P.replay("gpsimd", e)

    return nc


_NC_CACHE = [None]


def kernel(L0_attn_x, L0_attn_Wq, L0_attn_bq, L0_attn_Wk, L0_attn_bk,
           L0_attn_Wv, L0_attn_bv, L0_attn_Wo, L0_attn_bo,
           L0_ffn_W_fc, L0_ffn_b_fc, L0_ffn_A_fc, L0_ffn_B_fc,
           L0_ffn_W_proj, L0_ffn_b_proj, L0_ffn_A_proj, L0_ffn_B_proj):
    f = np.ascontiguousarray
    x = np.asarray(L0_attn_x, dtype=np.float32)

    shared = {
        "wfc_t": f(np.asarray(L0_ffn_W_fc, np.float32).T),
        "wpj_t": f(np.asarray(L0_ffn_W_proj, np.float32).T),
        "a_fc": f(np.asarray(L0_ffn_A_fc, np.float32)),
        "b_fc_t": f(np.asarray(L0_ffn_B_fc, np.float32).T),
        "a_pj": f(np.asarray(L0_ffn_A_proj, np.float32)),
        "b_pj_t": f(np.asarray(L0_ffn_B_proj, np.float32).T),
        "wq": f(np.asarray(L0_attn_Wq, np.float32).reshape(D, D)),
        "wk": f(np.asarray(L0_attn_Wk, np.float32).reshape(D, D)),
        "wv": f(np.asarray(L0_attn_Wv, np.float32).reshape(D, D)),
        "wo": f(np.asarray(L0_attn_Wo, np.float32).reshape(D, D)),
        "bias_fc": f(np.asarray(L0_ffn_b_fc, np.float32).reshape(FC, 128)),
        "bias_pj": f(np.asarray(L0_ffn_b_proj, np.float32).reshape(KC, 128)),
        "bias_o": f(np.asarray(L0_attn_bo, np.float32).reshape(KC, 128)),
        "bias_q": f(np.asarray(L0_attn_bq, np.float32).reshape(KC, 128)),
        "bias_k": f(np.asarray(L0_attn_bk, np.float32).reshape(1, D)),
        "bias_v": f(np.asarray(L0_attn_bv, np.float32).reshape(1, D)),
        "eye": np.eye(128, dtype=np.float32),
    }
    ebc = np.zeros((H, KC * 128), dtype=np.float32)
    for c in range(KC):
        ebc[2 * c, c * 128:c * 128 + 64] = 1.0
        ebc[2 * c + 1, c * 128 + 64:(c + 1) * 128] = 1.0
    shared["e_bc"] = ebc

    if _NC_CACHE[0] is None:
        _NC_CACHE[0] = build_nc()
    nc = _NC_CACHE[0]

    in_maps = []
    for c in range(NCORES):
        b, hh = c // 2, c % 2
        m = dict(shared)
        m["x_t"] = f(x[b, hh * T:(hh + 1) * T, :].T)
        in_maps.append(m)

    res = run_bass_kernel_spmd(nc, in_maps, core_ids=list(range(NCORES)))

    out = np.empty((4, 8192, D), dtype=np.float32)
    for c in range(NCORES):
        b, hh = c // 2, c % 2
        out[b, hh * T:(hh + 1) * T, :] = res.results[c]["out_x"].T
    return out


# version 19 (final)
# speedup vs baseline: 1.0298x; 1.0298x over previous
import contextlib
import numpy as np

import concourse.bass as bass
import concourse.mybir as mybir
from concourse.bass_utils import run_bass_kernel_spmd

F32 = mybir.dt.float32
BF16 = mybir.dt.bfloat16
I32 = mybir.dt.int32
ALU = mybir.AluOpType
ACT = mybir.ActivationFunctionType
AX = mybir.AxisListType

NCORES = 8
D = 1024
F = 4096
T = 4096
NT = 8
TW = 512
KC = D // 128
FC = F // 128
H = 16
LN_EPS = 1e-5
ATT_EPS = 1e-6
R = 16


class Sem:
    def __init__(self, handle):
        self.h = handle
        self.n = 0


class Prog:

    def __init__(self):
        self.items = []
        self.waited = {}
        self.chain_sems = {}

    def op(self, eng, builder, waits=(), inc=None, inc_k=1, chain=None):
        cs = self.chain_sems.get(eng)
        if chain is None:
            chain = cs is not None
        waits = list(waits)
        if chain and cs is not None:
            waits.append((cs, cs.n))
            if inc is None:
                inc, inc_k = cs, 1
            else:
                assert inc is cs, f"chained op on {eng} must inc its chain sem"
        cap = []
        for s, v in waits:
            if v <= 0:
                continue
            key = (eng, id(s))
            if self.waited.get(key, -1) >= v:
                continue
            self.waited[key] = v
            cap.append((s.h, v))
        tgt = None
        if inc is not None:
            inc.n += inc_k
            tgt = (inc.h, inc_k)

        def fn(e, _cap=cap, _b=builder, _tgt=tgt):
            for h, v in _cap:
                e.wait_ge(h, v)
            ins = _b(e)
            if _tgt is not None:
                ins.then_inc(*_tgt)
        self.items.append((eng, fn))

    def replay(self, eng_name, e):
        for name, fn in self.items:
            if name == eng_name:
                fn(e)


DEBUG = False
PHASES = 4


def build_nc(debug=None, phases=None):
    if debug is None:
        debug = DEBUG
    if phases is None:
        phases = PHASES
    nc = bass.Bass("TRN2", target_bir_lowering=False)

    dr = lambda name, shape, dt=F32: nc.dram_tensor(name, shape, dt, kind="ExternalInput").ap()
    x_in = dr("x_t", [D, T])
    wfc_t = dr("wfc_t", [D, F])
    wpj_t = dr("wpj_t", [F, D])
    afc_in = dr("a_fc", [R, D])
    bfc2_in = dr("b_fc_t", [R, F])
    apj_in = dr("a_pj", [R, F])
    bpj2_in = dr("b_pj_t", [R, D])
    wq_in = dr("wq", [D, D])
    wk_in = dr("wk", [D, D])
    wv_in = dr("wv", [D, D])
    wo_in = dr("wo", [D, D])
    bfc_in = dr("bias_fc", [FC, 128])
    bpj_in = dr("bias_pj", [KC, 128])
    bo_in = dr("bias_o", [KC, 128])
    bq_in = dr("bias_q", [KC, 128])
    bk_in = dr("bias_k", [1, D])
    bv_in = dr("bias_v", [1, D])
    eye_in = dr("eye", [128, 128])
    ebc_in = dr("e_bc", [H, KC * 128])

    out_x = nc.dram_tensor("out_x", [D, T], F32, kind="ExternalOutput").ap()

    scr_fc = nc.dram_tensor("scr_fc", [FC, 128, KC * 128], BF16).ap()
    scr_pj = nc.dram_tensor("scr_pj", [KC, 128, FC * 128], BF16).ap()
    ln2_dram = nc.dram_tensor("ln2_dram", [2, T], F32).ap()
    ln3_dram = nc.dram_tensor("ln3_dram", [2, T], F32).ap()
    cc_in = nc.dram_tensor("cc_in", [128, 520], F32)
    cc_out = nc.dram_tensor("cc_out", [128, 520], F32)
    if debug:
        dbg_x1 = nc.dram_tensor("dbg_x1", [D, T], F32, kind="ExternalOutput").ap()
        dbg_x2 = nc.dram_tensor("dbg_x2", [D, T], F32, kind="ExternalOutput").ap()
        dbg_kv = nc.dram_tensor("dbg_kv", [128, 520], F32, kind="ExternalOutput").ap()
        dbg_wfc = nc.dram_tensor("dbg_wfc", [FC, 128, KC * 128], BF16,
                                 kind="ExternalOutput").ap()
        dbg_wpj = nc.dram_tensor("dbg_wpj", [KC, 128, FC * 128], BF16,
                                 kind="ExternalOutput").ap()
        dbg_ln2 = nc.dram_tensor("dbg_ln2", [2, T], F32, kind="ExternalOutput").ap()
        dbg_qf = nc.dram_tensor("dbg_qf", [128, KC * TW], BF16, kind="ExternalOutput").ap()
        dbg_z = nc.dram_tensor("dbg_z", [H, TW], F32, kind="ExternalOutput").ap()
        dbg_y = nc.dram_tensor("dbg_y", [128, KC * TW], BF16, kind="ExternalOutput").ap()
        dbg_ln5 = nc.dram_tensor("dbg_ln5", [128, KC * TW], BF16, kind="ExternalOutput").ap()

    P = Prog()

    with contextlib.ExitStack() as ctx:
        avail = nc.sbuf_bytes_remaining
        words = (avail - 1024) // 4
        arena = ctx.enter_context(nc.sbuf_tensor("arena", [128, words], F32))
        _off = [0]

        def reg(nbytes, align=4):
            a = (_off[0] + align - 1) // align * align
            _off[0] = a + nbytes
            return a

        def f32v(off, n, p0=0, pn=128):
            assert off % 4 == 0
            return arena[p0:p0 + pn, off // 4: off // 4 + n]

        def bf16v(off, n, p0=0, pn=128):
            assert off % 4 == 0 and n % 2 == 0
            return arena[p0:p0 + pn, off // 4: off // 4 + n // 2].bitcast(BF16)

        def i32v(off, n, p0=0, pn=128):
            return arena[p0:p0 + pn, off // 4: off // 4 + n].bitcast(I32)

        pb = [ctx.enter_context(nc.psum_tensor(f"pb{i}", [128, 512], F32)) for i in range(8)]

        mk = lambda name: Sem(ctx.enter_context(nc.semaphore(name)))
        sT = mk("sT"); sV = mk("sV"); sS = mk("sS"); sG = mk("sG")
        sCC = mk("sCC"); dOUT = mk("dOUT")
        sPRO = mk("sPRO")
        sSTGl = [mk("sSTGa"), mk("sSTGb")]
        sX = mk("sX")
        sSCW = mk("sSCW")
        dWa = mk("dWa"); dWb = mk("dWb"); dWc = mk("dWc")
        dPa = mk("dPa"); dPb = mk("dPb")
        sROW = mk("sROW")
        sRST = mk("sRST")
        gW = mk("gW")
        gEBC = mk("gEBC")
        gCC = mk("gCC")
        gOUT = mk("gOUT")
        sXl = [mk(f"sX{i}") for i in range(KC)]
        P.chain_sems = {"vector": sV, "scalar": sS, "gpsimd": sG}

        o_eye = reg(128 * 4)
        o_bfc = reg(FC * 4)
        o_bpj = reg(KC * 4)
        o_bpjh = reg(KC * 4)
        o_bo = reg(KC * 4)
        o_bq = reg(KC * 4)
        o_ones_cbf = reg(8)
        o_ones_rbf = reg(128 * 2)
        o_ones_rf = reg(128 * 4)
        o_sc = reg(8)
        o_inv = reg(8)
        o_half = reg(8)
        o_zero = reg(8)
        o_amaxc = reg(8)
        o_srow = reg(128 * 4)
        o_rows = reg(4 * 512 * 4)
        o_btmp = reg(128 * 4)

        eye = f32v(o_eye, 128)
        bfc_col = f32v(o_bfc, FC)
        bpj_col = f32v(o_bpj, KC)
        bpjh_col = f32v(o_bpjh, KC)
        bo_col = f32v(o_bo, KC)
        bq_col = f32v(o_bq, KC)
        ones_cbf = bf16v(o_ones_cbf, 2)[:, 0:1]
        ones_rbf = bf16v(o_ones_rbf, 128, 0, 1)
        ones_rf = f32v(o_ones_rf, 128, 0, 1)
        sc_col = f32v(o_sc, 1)
        inv_col = f32v(o_inv, 1)
        half_col = f32v(o_half, 1)
        zero_col = f32v(o_zero, 1)
        amax_col = f32v(o_amaxc, 1)
        srow = f32v(o_srow, 128, 0, 1)
        rowm = f32v(o_rows, 512, 0, 1)
        rowv = f32v(o_rows + 2048, 512, 0, 1)
        rowa = f32v(o_rows + 4096, 512, 0, 1)
        rowb = f32v(o_rows + 6144, 512, 0, 1)

        o_dyn = _off[0]

        o_stage = o_dyn
        o_weff = o_stage + 2 * 8192 * 4
        o_qt = o_weff + 32768 * 2
        o_qq = o_qt + 2048 * 4
        o_qc = o_qq + 2048 * 4
        o_afc = o_qc + 2048 * 4
        o_apj = o_afc + D * 2
        o_b2fc = o_apj + F * 2
        o_b2pj = o_b2fc + F * 2
        o_qt2 = o_b2pj + D * 2
        o_qq2 = o_qt2 + 2048 * 4
        o_qc2 = o_qq2 + 2048 * 4
        pro_end = o_qc2 + 2048 * 4

        stage = lambda w0, n, h=0: f32v(o_stage + (h * 8192 + w0) * 4, n)
        weff_flat = bf16v(o_weff, 32768)
        qset = [(f32v(o_qt, 2048), f32v(o_qq, 2048), f32v(o_qc, 2048),
                 i32v(o_qc, 2048), o_qt),
                (f32v(o_qt2, 2048), f32v(o_qq2, 2048), f32v(o_qc2, 2048),
                 i32v(o_qc2, 2048), o_qt2)]
        afc_bf = bf16v(o_afc, D, 0, R)
        apj_bf = bf16v(o_apj, F, 0, R)
        b2fc = bf16v(o_b2fc, F, 0, R)
        b2pj = bf16v(o_b2pj, D, 0, R)
        b2st_fc = f32v(o_qt2, F, 0, R)
        b2st_pj = f32v(o_qt2, D, 0, R)

        o_X = o_dyn
        o_h = o_X + 32768 * 4
        o_ln = o_h + FC * TW * 2
        o_t1 = o_ln + KC * TW * 2
        o_xbf = o_t1 + 512 * 4
        o_x2bf = o_xbf + 1024 * 2
        o_wbuf = o_x2bf + 1024 * 2
        o_wpbuf = o_wbuf + 2048 * 3
        o_lnt1 = o_xbf
        main_end = o_wpbuf + 8192 * 2

        Xv = lambda c, t0, n: f32v(o_X + (c * T + t0) * 4, n)
        hv = lambda o: bf16v(o_h + o * TW * 2, TW)
        lnv = lambda c, j0, n: bf16v(o_ln + (c * TW + j0) * 2, n)
        t1v = f32v(o_t1, 512)
        xbf = lambda i: bf16v(o_xbf + i * 1024, 512)
        x2bf = lambda i: bf16v(o_x2bf + i * 1024, 512)
        wb = lambda i: bf16v(o_wbuf + i * 2048, 1024)
        wpb = lambda i: bf16v(o_wpbuf + i * 8192, 4096)
        lnt1 = f32v(o_lnt1, 512)

        o_w1 = o_h
        o_w2 = o_w1 + 8192 * 2
        o_ln5 = o_w2 + 8192 * 2
        o_at1 = o_ln5 + KC * TW * 2
        o_at2 = o_at1 + 512 * 4
        o_at3 = o_at2 + 512 * 4
        o_ebc = o_at3 + 512 * 4
        o_kvf = o_ebc + 1024 * 4
        o_kvbf = o_kvf + 520 * 4
        o_zd = o_kvbf + 520 * 2 + 4
        o_zrow = o_zd + 128 * 2
        o_swp = o_zrow + 512 * 4
        o_kf = o_swp
        o_vext = o_kf + 1024 * 2
        o_bkr = o_vext + 1040 * 2
        o_bvr = o_bkr + D * 2
        o_qf = o_swp
        o_ysb = o_qf
        o_zbc = o_swp + KC * TW * 2
        o_kvbd = o_zbc + 512 * 4
        attn_end = o_kvbd + KC * 128 * 2

        w1 = lambda w0, n: bf16v(o_w1 + w0 * 2, n)
        w2 = lambda w0, n: bf16v(o_w2 + w0 * 2, n)
        ln5 = lambda c, j0, n: bf16v(o_ln5 + (c * TW + j0) * 2, n)
        at1 = f32v(o_at1, 512)
        at2 = f32v(o_at2, 512)
        at3 = f32v(o_at3, 512)
        axbf = lambda i: bf16v(o_at1 + i * 1024, 512)
        ax2bf = lambda i: bf16v(o_at2 + i * 1024, 512)
        ebc = f32v(o_ebc, KC * 128, 0, H)
        kv_f = f32v(o_kvf, 520)
        sx2b = lambda i: bf16v(o_kvf + i * 1024, 512)
        kv_bf = bf16v(o_kvbf, 520)
        zdiag = bf16v(o_zd, 128)
        zrow = f32v(o_zrow, 512, 0, H)
        kf_sb = lambda h0, n: bf16v(o_kf + h0 * 2, n)
        vext = bf16v(o_vext, 1040)
        bk_row = bf16v(o_bkr, D, 0, 1)
        bv_row = bf16v(o_bvr, D, 0, 1)
        qf_sb = lambda c: bf16v(o_qf + c * TW * 2, TW)
        y_sb = lambda c: bf16v(o_ysb + c * TW * 2, TW)
        zbc_sb = f32v(o_zbc, 512)
        sxb = lambda i: bf16v(o_zbc + i * 1024, 512)
        kvbd = bf16v(o_kvbd, KC * 128)

        assert max(pro_end, main_end, attn_end) <= words * 4, (
            pro_end, main_end, attn_end, words * 4)

        def dma(eng, sem, out, in_, waits=(), serial=False):
            if serial:
                waits = list(waits) + [(sem, sem.n)]
            P.op(eng, lambda e, o=out, i=in_: e.dma_start(out=o, in_=i),
                 waits=waits, inc=sem, inc_k=16, chain=False)

        dma("sync", sPRO, eye, eye_in, serial=True)
        dma("gpsimd", gW, afc_bf, afc_in)
        dma("gpsimd", gW, apj_bf, apj_in)
        ab_loaded = gW.n
        dma("sync", sPRO, b2st_fc, bfc2_in, serial=True)
        w_ = sPRO.n
        P.op("vector", lambda e: e.tensor_scalar(b2fc, b2st_fc, 2.0, None, ALU.mult),
             waits=[(sPRO, w_)], inc=sV)
        b2fc_v = sV.n
        dma("sync", sPRO, b2st_pj, bpj2_in, waits=[(sV, b2fc_v)], serial=True)
        w_ = sPRO.n
        P.op("vector", lambda e: e.tensor_scalar(b2pj, b2st_pj, 1.0, None, ALU.mult),
             waits=[(sPRO, w_)], inc=sV)
        b2_v = sV.n
        P.op("vector", lambda e: e.memset(ones_cbf, 1.0), inc=sV)
        P.op("vector", lambda e: e.memset(ones_rbf, 1.0), inc=sV)
        P.op("vector", lambda e: e.memset(ones_rf, 1.0), inc=sV)
        P.op("vector", lambda e: e.memset(half_col, 0.5), inc=sV)
        P.op("vector", lambda e: e.memset(zero_col, 0.0), inc=sV)

        def bias_cols(src_dram, nchunks, dst_col):
            t = f32v(o_btmp, 128, 0, nchunks)
            dma("sync", sPRO, t, src_dram, waits=[(sV, sV.n), (sT, sT.n)], serial=True)
            wd = sPRO.n
            P.op("tensor", lambda e, t=t, n=nchunks: e.transpose(
                pb[0][:, 0:n], t, eye[0:n, 0:n]),
                waits=[(sPRO, wd), (sV, sV.n)], inc=sT)
            P.op("vector", lambda e, d=dst_col, n=nchunks: e.tensor_copy(
                d, pb[0][:, 0:n]), waits=[(sT, sT.n)], inc=sV)

        bias_cols(bfc_in, FC, bfc_col)
        bias_cols(bpj_in, KC, bpj_col)
        bias_cols(bo_in, KC, bo_col)
        bias_cols(bq_in, KC, bq_col)
        P.op("vector", lambda e: e.tensor_scalar(bpjh_col, bpj_col, 0.5, None, ALU.mult),
             inc=sV)

        ba_ev = {}
        uslot = [0]
        qstate = {"uidx": 0, "evac": [0, 0], "act2": [0, 0], "pend": [None]}

        def quant_weight(w_dram, nchunks, chunk_w, a_bf, b2_bf, scr_ap, nout,
                         prev_scr, sc_fac=1.0):
            grp_ch = 8192 // chunk_w
            ngrp = nchunks // grp_ch
            kj = nchunks * 128

            ba_done = {}
            nslice = nchunks * chunk_w // 512
            for i in range(nslice):
                k = (i * 512) // chunk_w
                col0 = (i * 512) % chunk_w
                bank = pb[2 + i % 4]
                w = [(gW, ab_loaded), (sV, b2_v)]
                if i - 4 in ba_done:
                    w.append((sS, ba_done[i - 4]))
                if i == 0 and prev_scr:
                    w.append((sSCW, prev_scr))
                P.op("tensor", lambda e, k=k, col0=col0, bank=bank, a_bf=a_bf,
                     b2_bf=b2_bf: e.matmul(
                        bank[:, :], a_bf[:, k * 128:(k + 1) * 128],
                        b2_bf[:, col0:col0 + 512], start=True, stop=True,
                        skip_group_check=True), waits=w, inc=sT)
                wt = sT.n
                base_o = col0 // 128
                dst = weff_flat.rearrange("p (o kj) -> p o kj", kj=kj)[
                    :, base_o:base_o + 4, k * 128:(k + 1) * 128]
                sw = [(sT, wt)]
                if i == 0 and prev_scr:
                    sw.append((sSCW, prev_scr))
                P.op("scalar", lambda e, dst=dst, bank=bank: e.copy(
                    dst, bank[:, :].rearrange("p (o j) -> p o j", j=128)),
                    waits=sw, inc=sS)
                ba_done[i] = sS.n
            ba_s = sS.n

            red_at = {}
            for g in range(ngrp):
                h = g % 2
                lane = sSTGl[h]
                for ci in range(grp_ch):
                    k = g * grp_ch + ci
                    w = []
                    if ci == 0:
                        w = ([(sV, red_at[g - 2])] if g - 2 in red_at
                             else [(sV, sV.n), (sG, sG.n)])
                    dma("sync", lane, stage(ci * chunk_w, chunk_w, h),
                        w_dram[k * 128:(k + 1) * 128, :], waits=w)
                wd = lane.n
                if g == 0:
                    P.op("vector", lambda e, h=h: e.reduce_max(
                        amax_col, stage(0, 8192, h), AX.X, apply_absolute_value=True),
                        waits=[(lane, wd)], inc=sV)
                else:
                    P.op("vector", lambda e, h=h: e.reduce_max(
                        f32v(o_amaxc + 4, 1), stage(0, 8192, h), AX.X,
                        apply_absolute_value=True), waits=[(lane, wd)], inc=sV)
                    P.op("vector", lambda e: e.tensor_tensor(
                        amax_col, amax_col, f32v(o_amaxc + 4, 1), ALU.max), inc=sV)
                red_at[g] = sV.n
            wv = sV.n
            P.op("tensor", lambda e: e.transpose(pb[0][0:1, 0:128], amax_col, eye),
                 waits=[(sV, wv), (sPRO, sPRO.n)], inc=sT)
            P.op("vector", lambda e: e.reduce_max(
                srow[0:1, 0:1], pb[0][0:1, 0:128], AX.X, apply_absolute_value=True),
                waits=[(sT, sT.n)], inc=sV)
            P.op("vector", lambda e: e.memset(srow[0:1, 4:5], 127.0), inc=sV)
            P.op("vector", lambda e: e.reciprocal(srow[0:1, 5:6], srow[0:1, 4:5]), inc=sV)
            P.op("vector", lambda e: e.tensor_tensor(
                srow[0:1, 1:2], srow[0:1, 0:1], srow[0:1, 5:6], ALU.mult), inc=sV)
            P.op("vector", lambda e: e.reciprocal(srow[0:1, 2:3], srow[0:1, 1:2]), inc=sV)
            wv = sV.n
            P.op("tensor", lambda e: e.matmul(
                pb[0][:, 0:2], ones_rf, srow[0:1, 1:3], start=True, stop=True,
                skip_group_check=True), waits=[(sV, wv)], inc=sT)
            P.op("vector", lambda e: e.tensor_scalar(
                sc_col, pb[0][:, 0:1], sc_fac, None, ALU.mult),
                waits=[(sT, sT.n)], inc=sV)
            P.op("vector", lambda e: e.tensor_copy(inv_col, pb[0][:, 1:2]), inc=sV)
            scale_v = sV.n

            units_per_grp = 8192 // 2048
            unit_at = {}
            for g in range(ngrp):
                h = g % 2
                lane = sSTGl[h]
                for ci in range(grp_ch):
                    k = g * grp_ch + ci
                    w = []
                    if ci == 0:
                        w = ([(sS, unit_at[g - 2])] if g - 2 in unit_at
                             else [(sV, sV.n), (sS, sS.n)])
                    dma("sync", lane, stage(ci * chunk_w, chunk_w, h),
                        w_dram[k * 128:(k + 1) * 128, :], waits=w)
                wd = lane.n
                for u in range(units_per_grp):
                    si = qstate["uidx"] % 2
                    qt, qq, qc, qi, o_qtX = qset[si]
                    st = stage(u * 2048, 2048, h)
                    sw = [(lane, wd), (sV, scale_v)]
                    if qstate["evac"][si]:
                        sw.append((sG, qstate["evac"][si]))
                    P.op("scalar", lambda e, st=st, qt=qt: e.activation(
                        qt, st, ACT.Identity, bias=half_col, scale=inv_col),
                        waits=sw, inc=sS)
                    s1_cnt = sS.n
                    if qstate["pend"][0] is not None:
                        qstate["pend"][0]()
                        qstate["pend"][0] = None
                    vw = [(sS, s1_cnt)]
                    if qstate["act2"][si]:
                        vw.append((sS, qstate["act2"][si]))
                    P.op("vector", lambda e, qi=qi, qt=qt: e.tensor_copy(qi, qt),
                         waits=vw)
                    P.op("vector", lambda e, qq=qq, qi=qi: e.tensor_copy(qq, qi))
                    P.op("vector", lambda e, qc=qc, qq=qq, qt=qt: e.tensor_tensor(
                        qc, qq, qt, ALU.is_gt))
                    P.op("vector", lambda e, qq=qq, qc=qc: e.tensor_tensor(
                        qq, qq, qc, ALU.subtract))
                    sub_cnt = sV.n

                    pieces = []
                    for piece in range(2):
                        gcol = g * 8192 + u * 2048 + piece * 1024
                        kk = gcol // chunk_w
                        s = (gcol % chunk_w) // 1024
                        base_o = (s * 1024) // 128
                        dst = weff_flat.rearrange("p (o kj) -> p o kj", kj=kj)[
                            :, base_o:base_o + 8, kk * 128:(kk + 1) * 128]
                        qsrc = f32v(o_qtX + piece * 1024 * 4, 1024).rearrange(
                            "p (o j) -> p o j", j=128)
                        pieces.append((dst, qsrc))

                    def mk_tail(qt=qt, qq=qq, sub_cnt=sub_cnt, si=si,
                                pieces=pieces):
                        def emit():
                            P.op("scalar", lambda e, qt=qt, qq=qq: e.activation(
                                qt, qq, ACT.Identity, bias=zero_col, scale=sc_col),
                                waits=[(sV, sub_cnt)], inc=sS)
                            a2 = sS.n
                            qstate["act2"][si] = a2
                            for j, (dst, qsrc) in enumerate(pieces):
                                P.op("gpsimd", lambda e, dst=dst, qsrc=qsrc:
                                     e.tensor_tensor(dst, dst, qsrc, ALU.add),
                                     waits=[(sS, a2)] if j == 0 else ())
                            qstate["evac"][si] = sG.n
                        return emit

                    qstate["pend"][0] = mk_tail()
                    qstate["uidx"] += 1
                unit_at[g] = sS.n
            if qstate["pend"][0] is not None:
                qstate["pend"][0]()
                qstate["pend"][0] = None
            wv = sV.n
            wg = sG.n
            for o in range(nout):
                src_ = bf16v(o_weff + o * kj * 2, kj)
                dma("sync", sSCW, scr_ap[o], src_,
                    waits=[(sV, wv), (sG, wg), (sS, sS.n)] if o == 0 else ())

        quant_weight(wfc_t, KC, F, afc_bf, b2fc, scr_fc, FC, 0)
        fc_scr_done = sSCW.n
        quant_weight(wpj_t, FC, D, apj_bf, b2pj, scr_pj, KC, fc_scr_done,
                     sc_fac=0.5)
        scr_all = sSCW.n

        for k in range(KC):
            if k >= 4:
                w = [(sSCW, scr_all)]
            else:
                w = [(sV, sV.n), (sG, sG.n), (sS, sS.n)]
            dma("sync", sXl[k], Xv(k, 0, T), x_in[k * 128:(k + 1) * 128, :], waits=w)

        def ln_stats(t0, xwaits_fn, xb, x2b, bankS=None, bankQ=None,
                     prowS=0, prowQ=0, cast_eng="vector"):
            bankS = pb[0] if bankS is None else bankS
            bankQ = pb[1] if bankQ is None else bankQ
            tmm = {}
            smm = {}
            t_at_call = sT.n
            s_at_call = sS.n
            for c in range(KC):
                i = c % 2
                w = list(xwaits_fn(c))
                if c - 2 in tmm:
                    w += [(sT, tmm[c - 2]), (sS, smm[c - 2])]
                else:
                    w += [(sT, t_at_call), (sS, s_at_call)]
                if cast_eng == "vector":
                    P.op("vector", lambda e, c=c, i=i, t0=t0: e.tensor_copy(
                        xb(i), Xv(c, t0, TW)), waits=w, inc=sV)
                    cast_w = (sV, sV.n)
                else:
                    P.op("scalar", lambda e, c=c, i=i, t0=t0: e.activation(
                        xb(i), Xv(c, t0, TW), ACT.Identity), waits=w, inc=sS)
                    cast_w = (sS, sS.n)
                wv = cast_w[1]
                sqw = [cast_w]
                if c - 2 in tmm:
                    sqw.append((sT, tmm[c - 2]))
                P.op("scalar", lambda e, i=i: e.activation(x2b(i), xb(i), ACT.Square),
                     waits=sqw, inc=sS)
                smm[c] = sS.n
                P.op("tensor", lambda e, c=c, i=i: e.matmul(
                    bankS[prowS:prowS + 1, :], ones_cbf, xb(i),
                    start=(c == 0), stop=(c == KC - 1),
                    tile_position=(0, prowS), skip_group_check=True),
                    waits=[cast_w], inc=None)
                P.op("tensor", lambda e, c=c, i=i: e.matmul(
                    bankQ[prowQ:prowQ + 1, :], ones_cbf, x2b(i),
                    start=(c == 0 and bankQ is not bankS), stop=(c == KC - 1),
                    tile_position=(0, prowQ), skip_group_check=True),
                    waits=[(sS, smm[c])], inc=sT)
                tmm[c] = sT.n
            wt = tmm[KC - 1]
            P.op("vector", lambda e: e.tensor_scalar(
                rowm, bankS[prowS:prowS + 1, :], 1.0 / D, None, ALU.mult),
                waits=[(sT, wt)], inc=sV)
            P.op("vector", lambda e: e.tensor_scalar(
                rowv, bankQ[prowQ:prowQ + 1, :], 1.0 / D, LN_EPS, ALU.mult, ALU.add),
                inc=sV)
            P.op("vector", lambda e: e.tensor_tensor(rowb, rowm, rowm, ALU.mult),
                 waits=[(sRST, sRST.n)], inc=sV)
            P.op("vector", lambda e: e.tensor_tensor(rowv, rowv, rowb, ALU.subtract), inc=sV)
            P.op("vector", lambda e: e.reciprocal(rowb, rowv), inc=sV)
            wv = sV.n
            P.op("scalar", lambda e: e.activation(rowa, rowb, ACT.Sqrt),
                 waits=[(sV, wv), (sRST, sRST.n)], inc=sS)
            ws = sS.n
            P.op("vector", lambda e: e.tensor_tensor(rowb, rowm, rowa, ALU.mult),
                 waits=[(sS, ws)], inc=sV)
            return sV.n

        def ln_bcast(rows_v, a_src, b_src, bankA, bankB, extra=()):
            w = [(sV, rows_v)] + list(extra)
            P.op("tensor", lambda e, a=a_src, bankA=bankA: e.matmul(
                bankA[:, :], ones_rf, a, start=True, stop=True,
                skip_group_check=True), waits=w, inc=sT)
            P.op("tensor", lambda e, b=b_src, bankB=bankB: e.matmul(
                bankB[:, :], ones_rf, b, start=True, stop=True,
                skip_group_check=True), inc=sT)
            return sT.n

        def ln_apply(t0, bc_t, bankA, bankB, dst_fn, scratch=None, dst_free_t=0):
            sc = t1v if scratch is None else scratch
            for c in range(KC):
                P.op("vector", lambda e, c=c, t0=t0, bankA=bankA, sc=sc: e.tensor_tensor(
                    sc, Xv(c, t0, TW), bankA[:, :], ALU.mult),
                    waits=[(sT, max(bc_t, dst_free_t))], inc=sV)
                P.op("vector", lambda e, c=c, bankB=bankB, d=dst_fn(c), sc=sc: e.tensor_tensor(
                    d, sc, bankB[:, :], ALU.subtract), inc=sV)
            return sV.n

        def ln_apply_chunks(t0, bc_t, bankA, bankB, dst_fn, dst_free_t=0):
            ats = []
            for c in range(KC):
                P.op("vector", lambda e, c=c, t0=t0, bankA=bankA: e.tensor_tensor(
                    t1v, Xv(c, t0, TW), bankA[:, :], ALU.mult),
                    waits=[(sT, max(bc_t, dst_free_t))], inc=sV)
                P.op("vector", lambda e, c=c, bankB=bankB, d=dst_fn(c): e.tensor_tensor(
                    d, t1v, bankB[:, :], ALU.subtract), inc=sV)
                ats.append(sV.n)
            return ats

        fcg = [0]; pjg = [0]
        fcmm_at = {}; fcev_at = {}; pjmm_at = {}; pjev_at = {}

        def ffn_pass(pass_idx):
            last_rv2 = [0]

            def stats_or_rows(tile, lnr_v=0):
                t0 = tile * TW
                if pass_idx == 0:
                    return ln_stats(t0, lambda c: [(sXl[c], 16)], xbf, x2bf)
                dma("sync", sROW, rowa, ln3_dram[0:1, t0:t0 + TW],
                    waits=[(sV, sV.n), (sT, sT.n), (sS, sS.n), (sRST, sRST.n)]
                    + ([(dOUT, dOUT.n)] if debug else []))
                dma("sync", sROW, rowb, ln3_dram[1:2, t0:t0 + TW])
                return 0

            def bcast(rv):
                extra = [(sROW, sROW.n)] if pass_idx == 1 else []
                return ln_bcast(rv, rowa, rowb, pb[2], pb[3], extra=extra)

            rv = stats_or_rows(0)
            bc = ln_bcast(rv, rowa, rowb, pb[2], pb[3],
                          extra=([(sROW, sROW.n), (sV, sV.n)] if pass_idx == 1 else []))
            lnr = ln_apply(0, bc, pb[2], pb[3], lambda c: lnv(c, 0, TW),
                           scratch=lnt1, dst_free_t=sT.n if pass_idx == 1 else 0)

            for tile in range(NT):
                t0 = tile * TW
                if tile + 1 < NT:
                    rv_n = stats_or_rows(tile + 1, lnr)
                for o in range(FC):
                    g = fcg[0]
                    buf = wb(g % 3)
                    lane = (dWa, dWb, dWc)[g % 3]
                    w = [(sSCW, scr_all)]
                    if g - 3 in fcmm_at:
                        w.append((sT, fcmm_at[g - 3]))
                    dma("sync", lane, buf, scr_fc[o], waits=w)
                    wd = lane.n
                    bank = pb[4 + (o % 2)]
                    w0 = [(lane, wd), (sV, lnr)]
                    if g - 2 in fcev_at:
                        w0.append((sS, fcev_at[g - 2]))
                    for k in range(KC):
                        P.op("tensor", lambda e, k=k, buf=buf, bank=bank: e.matmul(
                            bank[:, :], buf[:, k * 128:(k + 1) * 128], lnv(k, 0, TW),
                            start=(k == 0), stop=(k == KC - 1), skip_group_check=True),
                            waits=w0 if k == 0 else (), inc=sT if k == KC - 1 else None)
                    fcmm_at[g] = sT.n
                    P.op("scalar", lambda e, o=o, bank=bank: e.activation(
                        hv(o), bank[:, :], ACT.Gelu, bias=bfc_col[:, o:o + 1], scale=1.0),
                        waits=[(sT, fcmm_at[g])], inc=sS)
                    fcev_at[g] = sS.n
                    fcg[0] += 1
                    if o == 16 and tile + 1 < NT:
                        bc_n = bcast(rv_n)
                        fc_last_future = sT.n + (FC - 1 - o)
                        lnr_next = ln_apply((tile + 1) * TW, bc_n, pb[2], pb[3],
                                            lambda c: lnv(c, 0, TW), scratch=lnt1,
                                            dst_free_t=fc_last_future)

                for o in range(KC):
                    g = pjg[0]
                    lane = dPa if g % 2 == 0 else dPb
                    w = [(sSCW, scr_all)]
                    if g - 2 in pjmm_at:
                        w.append((sT, pjmm_at[g - 2]))
                    buf = wpb(g % 2)
                    dma("sync", lane, buf, scr_pj[o], waits=w)
                    wd = lane.n
                    bank = pb[6 + (o % 2)]
                    w0 = [(lane, wd), (sS, fcev_at[fcg[0] - 1])]
                    if g - 2 in pjev_at:
                        w0.append((sV, pjev_at[g - 2]))
                    for k in range(FC):
                        P.op("tensor", lambda e, k=k, buf=buf, bank=bank: e.matmul(
                            bank[:, :], buf[:, k * 128:(k + 1) * 128], hv(k),
                            start=(k == 0), stop=(k == FC - 1), skip_group_check=True),
                            waits=w0 if k == 0 else (), inc=sT if k == FC - 1 else None)
                    pjmm_at[g] = sT.n
                    P.op("vector", lambda e, o=o, t0=t0, bank=bank:
                         e.scalar_tensor_tensor(
                            Xv(o, t0, TW), bank[:, :], bpjh_col[:, o:o + 1],
                            Xv(o, t0, TW), ALU.add, ALU.add),
                        waits=[(sT, pjmm_at[g])], inc=sV)
                    pjev_at[g] = sV.n
                    if pass_idx == 1:
                        dma("gpsimd", gOUT, out_x[o * 128:(o + 1) * 128, t0:t0 + TW],
                            Xv(o, t0, TW), waits=[(sV, pjev_at[g])])
                    pjg[0] += 1

                if pass_idx == 0:
                    gbase = pjg[0] - KC
                    rv2 = ln_stats(t0, lambda c: [(sV, pjev_at[gbase + c])],
                                   xbf, x2bf)
                    last_rv2[0] = rv2
                    dma("sync", sRST, ln2_dram[0:1, t0:t0 + TW], rowa, waits=[(sV, rv2)])
                    dma("sync", sRST, ln2_dram[1:2, t0:t0 + TW], rowb)
                if tile + 1 < NT:
                    lnr = lnr_next

        ffn_pass(0)
        ffn1_v = sV.n
        ffn1_mm_t = pjmm_at[pjg[0] - 1]
        ffn1_ge_s = fcev_at[fcg[0] - 1]
        if debug:
            for k in range(KC):
                dma("sync", dOUT, dbg_x1[k * 128:(k + 1) * 128, :], Xv(k, 0, T),
                    waits=[(sV, ffn1_v)])
            dma("sync", dOUT, dbg_ln2, ln2_dram, waits=[(sRST, sRST.n)])
            for o in range(FC):
                dma("sync", dOUT, dbg_wfc[o], scr_fc[o])
            for o in range(KC):
                dma("sync", dOUT, dbg_wpj[o], scr_pj[o])
        ffn1_rst = sRST.n

        dma("sync", sPRO, ebc, ebc_in, waits=[(sT, ffn1_mm_t)], serial=True)
        ebc_d = sPRO.n
        dma("gpsimd", gW, bk_row, bk_in, waits=[(sT, ffn1_mm_t), (sS, ffn1_ge_s)])
        dma("gpsimd", gW, bv_row, bv_in)
        for k in range(KC):
            P.op("gpsimd", lambda e, k=k: e.dma_start(
                out=w1(k * 1024, 1024), in_=wk_in[k * 128:(k + 1) * 128, :]),
                waits=[(sT, ffn1_mm_t), (sS, ffn1_ge_s)], inc=gW, inc_k=16, chain=False)
            P.op("gpsimd", lambda e, k=k: e.dma_start(
                out=w2(k * 1024, 1024), in_=wv_in[k * 128:(k + 1) * 128, :]),
                inc=gW, inc_k=16, chain=False)
        wkv_d = gW.n
        P.op("vector", lambda e: e.memset(
            vext.rearrange("p (h m) -> p h m", m=65)[:, :, 64:65], 1.0),
            waits=[(sT, ffn1_mm_t)], inc=sV)
        vones_v = sV.n

        kv_started = [False, False]
        prev_kv_t = [0]
        prev_ex_s = [0]
        pend_kv = [None]
        ev_free = [0, 0, 0, 0]
        last_kmm = [0]
        last_vmm = [0]

        dma("sync", sROW, rowa, ln2_dram[0:1, 0:TW],
            waits=[(sRST, ffn1_rst), (sT, sT.n), (sV, sV.n), (sS, sS.n),
                   (dOUT, dOUT.n)])
        dma("sync", sROW, rowb, ln2_dram[1:2, 0:TW])
        bc = ln_bcast(0, rowa, rowb, pb[0], pb[1],
                      extra=[(sROW, sROW.n), (sV, vones_v), (sPRO, ebc_d)])
        apply_at = ln_apply_chunks(0, bc, pb[0], pb[1], lambda c: ln5(c, 0, TW))
        bc_t_cur = bc
        for g5 in range(NT):
            t0 = g5 * TW
            for mt in range(4):
                j0 = mt * 128
                is_last = (g5 == NT - 1 and mt == 3)
                for half in range(2):
                    for k in range(KC):
                        w = []
                        if mt == 0:
                            w.append((sV, apply_at[k]))
                        if k == 0:
                            w += [(gW, wkv_d), (sV, ev_free[half])]
                        P.op("tensor", lambda e, k=k, half=half, j0=j0: e.matmul(
                            pb[4 + half][:, :], ln5(k, j0, 128),
                            w1(k * 1024 + half * 512, 512),
                            start=(k == 0), stop=False, skip_group_check=True),
                            waits=w, inc=None)
                    P.op("tensor", lambda e, half=half: e.matmul(
                        pb[4 + half][:, :], ones_rbf,
                        bk_row[0:1, half * 512:half * 512 + 512],
                        start=False, stop=True, skip_group_check=True), inc=sT)
                kmm = sT.n
                last_kmm[0] = kmm
                for half in range(2):
                    for k in range(KC):
                        w = []
                        if mt == 0:
                            w.append((sV, apply_at[k]))
                        if k == 0:
                            w.append((sV, ev_free[2 + half]))
                        P.op("tensor", lambda e, k=k, half=half, j0=j0: e.matmul(
                            pb[6 + half][:, :], ln5(k, j0, 128),
                            w2(k * 1024 + half * 512, 512),
                            start=(k == 0), stop=False, skip_group_check=True),
                            waits=w, inc=None)
                    P.op("tensor", lambda e, half=half: e.matmul(
                        pb[6 + half][:, :], ones_rbf,
                        bv_row[0:1, half * 512:half * 512 + 512],
                        start=False, stop=True, skip_group_check=True), inc=sT)
                vmm = sT.n
                last_vmm[0] = vmm
                if pend_kv[0] is not None:
                    pend_kv[0]()
                    pend_kv[0] = None
                for half in range(2):
                    P.op("vector", lambda e, half=half: e.tensor_scalar(
                        at1, pb[4 + half][:, :], 0.0, None, ALU.min),
                        waits=[(sT, kmm), (sS, prev_ex_s[0])], inc=sV)
                    wv = sV.n
                    P.op("scalar", lambda e: e.activation(at2, at1, ACT.Exp),
                         waits=[(sV, wv)], inc=sS)
                    prev_ex_s[0] = sS.n
                    P.op("vector", lambda e, half=half: e.tensor_scalar(
                        at1, pb[4 + half][:, :], 0.0, None, ALU.max),
                        waits=[(sS, prev_ex_s[0])], inc=sV)
                    ev_free[half] = sV.n
                    P.op("vector", lambda e, half=half: e.tensor_tensor(
                        kf_sb(half * 512, 512), at2, at1, ALU.add),
                        waits=[(sT, prev_kv_t[0])], inc=sV)
                    dsth = vext.rearrange("p (h m) -> p h m", m=65)[
                        :, half * 8:(half + 1) * 8, 0:64]
                    srch = pb[6 + half][:, :].rearrange("p (h m) -> p h m", m=64)
                    P.op("vector", lambda e, dsth=dsth, srch=srch: e.tensor_copy(
                        dsth, srch), waits=[(sT, vmm)], inc=sV)
                    ev_free[2 + half] = sV.n
                kvr = sV.n

                def mk_kv(kvr=kvr, is_last=is_last):
                    def emit():
                        for hh in range(H):
                            c = hh // 2
                            bank = pb[2 + (c // 4)]
                            col0 = (c % 4) * 65
                            po = 64 * (hh % 2)
                            fb = not kv_started[c // 4]
                            kv_started[c // 4] = True
                            P.op("tensor", lambda e, hh=hh, bank=bank, col0=col0,
                                 po=po, fb=fb, last=is_last: e.matmul(
                                    bank[po:po + 64, col0:col0 + 65],
                                    kf_sb(hh * 64, 64), vext[:, hh * 65:(hh + 1) * 65],
                                    start=fb, stop=(last and hh >= H - 2),
                                    tile_position=(0, po), skip_group_check=True),
                                waits=[(sV, kvr)] if hh == 0 else (),
                                inc=sT if hh == H - 1 else None)
                        prev_kv_t[0] = sT.n
                    return emit

                pend_kv[0] = mk_kv()
            if g5 + 1 < NT:
                t1n = (g5 + 1) * TW
                dma("sync", sROW, rowa, ln2_dram[0:1, t1n:t1n + TW],
                    waits=[(sT, bc_t_cur)])
                dma("sync", sROW, rowb, ln2_dram[1:2, t1n:t1n + TW])
                bc = ln_bcast(apply_at[KC - 1], rowa, rowb, pb[0], pb[1],
                              extra=[(sROW, sROW.n)])
                bc_t_cur = bc
                pend_kv[0]()
                pend_kv[0] = None
                apply_at = ln_apply_chunks(t1n, bc, pb[0], pb[1],
                                           lambda c: ln5(c, 0, TW),
                                           dst_free_t=last_vmm[0])
        pend_kv[0]()
        pend_kv[0] = None
        kv_done_t = sT.n
        s1_lnr = sV.n

        for k in range(KC):
            P.op("gpsimd", lambda e, k=k: e.dma_start(
                out=w1(k * 1024, 1024), in_=wq_in[k * 128:(k + 1) * 128, :]),
                waits=[(sT, last_kmm[0])] if k == 0 else (),
                inc=gW, inc_k=16, chain=False)
        for k in range(KC):
            P.op("gpsimd", lambda e, k=k: e.dma_start(
                out=w2(k * 1024, 1024), in_=wo_in[k * 128:(k + 1) * 128, :]),
                waits=[(sT, last_vmm[0])] if k == 0 else (),
                inc=gW, inc_k=16, chain=False)
        wqo_d = gW.n

        P.op("vector", lambda e: e.tensor_copy(kv_f[:, 0:260], pb[2][:, 0:260]),
             waits=[(sT, kv_done_t)], inc=sV)
        P.op("vector", lambda e: e.tensor_copy(kv_f[:, 260:520], pb[3][:, 0:260]), inc=sV)
        kvevac_v = sV.n
        wv = sV.n
        P.op("gpsimd", lambda e: e.dma_start(out=cc_in.ap(), in_=kv_f),
             waits=[(sV, wv)], inc=gCC, inc_k=16, chain=False)
        P.op("gpsimd", lambda e: e.collective_compute(
            "AllReduce", ALU.add, replica_groups=[[0, 1], [2, 3], [4, 5], [6, 7]],
            ins=[cc_in.ap()], outs=[cc_out.ap()]),
            waits=[(gCC, gCC.n)], inc=sCC, chain=False)
        P.op("gpsimd", lambda e: e.dma_start(out=kv_f, in_=cc_out.ap()),
             waits=[(sCC, sCC.n)], inc=gCC, inc_k=16, chain=False)
        wg = gCC.n
        kvr_box = [0]

        def emit_kv_post():
            P.op("vector", lambda e: e.tensor_copy(kv_bf, kv_f), waits=[(gCC, wg)], inc=sV)
            P.op("vector", lambda e: e.memset(zdiag, 0.0), inc=sV)
            for c in range(KC):
                kcol = 260 * (c // 4) + (c % 4) * 65 + 64
                P.op("vector", lambda e, c=c, kcol=kcol: e.tensor_copy(
                    zdiag[0:64, c * 16 + 2 * c:c * 16 + 2 * c + 1],
                    kv_bf[0:64, kcol:kcol + 1]), inc=sV)
                P.op("vector", lambda e, c=c, kcol=kcol: e.tensor_copy(
                    zdiag[64:128, c * 16 + 2 * c + 1:c * 16 + 2 * c + 2],
                    kv_bf[64:128, kcol:kcol + 1]), inc=sV)
            P.op("vector", lambda e: e.memset(kvbd, 0.0), inc=sV)
            for c in range(KC):
                kcol = 260 * (c // 4) + (c % 4) * 65
                P.op("vector", lambda e, c=c, kcol=kcol: e.tensor_copy(
                    kvbd[0:64, c * 128:c * 128 + 64], kv_bf[0:64, kcol:kcol + 64]), inc=sV)
                P.op("vector", lambda e, c=c, kcol=kcol: e.tensor_copy(
                    kvbd[64:128, c * 128 + 64:(c + 1) * 128],
                    kv_bf[64:128, kcol:kcol + 64]), inc=sV)
            kvr_box[0] = sV.n


        dma("sync", sROW, rowa, ln2_dram[0:1, 0:TW],
            waits=[(sT, sT.n), (sV, s1_lnr), (sS, sS.n), (sRST, sRST.n)]
            + ([(dOUT, dOUT.n)] if debug else []))
        dma("sync", sROW, rowb, ln2_dram[1:2, 0:TW])
        bc = ln_bcast(0, rowa, rowb, pb[0], pb[1],
                      extra=[(sROW, sROW.n), (sV, s1_lnr)])
        lnr = ln_apply(0, bc, pb[0], pb[1], lambda c: ln5(c, 0, TW),
                       dst_free_t=kv_done_t)
        bc_cur = bc
        ysc_at = {}
        qact_c = {}

        def emit_q(lnr_v, oproj_gate_t, pb01_free_v=0):
            qv_at = {}
            qs_at = {}
            qact_c.clear()
            for c in range(KC):
                bank = pb[c % 2]
                w0 = [(gW, wqo_d), (sV, lnr_v)]
                if c == 0 and pb01_free_v:
                    w0.append((sV, pb01_free_v))
                if c - 2 in qact_c:
                    w0.append((sS, qact_c[c - 2]))
                for k in range(KC):
                    P.op("tensor", lambda e, c=c, k=k, bank=bank: e.matmul(
                        bank[:, :], w1(k * 1024 + c * 128, 128), ln5(k, 0, TW),
                        start=(k == 0), stop=(k == KC - 1), skip_group_check=True),
                        waits=w0 if k == 0 else (), inc=sT if k == KC - 1 else None)
                qmm = sT.n
                cw = [(sT, qmm)]
                if c - 1 in qv_at:
                    cw.append((sV, qv_at[c - 1]))
                P.op("scalar", lambda e, c=c, bank=bank: e.activation(
                    at1, bank[:, :], ACT.Identity, bias=bq_col[:, c:c + 1], scale=1.0),
                    waits=cw, inc=sS)
                qact_c[c] = sS.n
                P.op("vector", lambda e: e.tensor_scalar(at2, at1, 0.0, None, ALU.min),
                     waits=[(sS, qact_c[c])], inc=sV)
                wv = sV.n
                P.op("scalar", lambda e: e.activation(at3, at2, ACT.Exp),
                     waits=[(sV, wv)], inc=sS)
                qs_at[c] = sS.n
                P.op("vector", lambda e: e.tensor_scalar(at1, at1, 0.0, None, ALU.max),
                     waits=[(sS, qs_at[c])], inc=sV)
                aw = [(sT, oproj_gate_t)] if c == 0 else []
                P.op("vector", lambda e, c=c: e.tensor_tensor(
                    qf_sb(c), at3, at1, ALU.add),
                    waits=aw + ([(dOUT, dOUT.n)] if debug else []), inc=sV)
                qv_at[c] = sV.n
            return sT.n, sV.n

        qmm_last, qf_v = emit_q(lnr, kv_done_t)
        emit_kv_post()
        if debug:
            dma("sync", dOUT, dbg_qf, bf16v(o_qf, KC * TW), waits=[(sV, qf_v)])
            dma("sync", dOUT, dbg_ln5, bf16v(o_ln5, KC * TW),
                waits=[(sV, qf_v), (sT, qmm_last)])
        prev_rv = 0
        for tile in range(NT):
            t0 = tile * TW
            kvr_v = kvr_box[0]
            for c in range(KC):
                P.op("tensor", lambda e, c=c: e.matmul(
                    pb[6][0:16, :], zdiag[:, c * 16:(c + 1) * 16], qf_sb(c),
                    start=(c == 0), stop=(c == KC - 1), skip_group_check=True),
                    waits=[(sV, qf_v), (sV, kvr_v)] if c == 0 else (),
                    inc=sT if c == KC - 1 else None)
            zmm = sT.n
            P.op("vector", lambda e: e.tensor_scalar(
                zrow, pb[6][0:16, :], ATT_EPS, None, ALU.add), waits=[(sT, zmm)], inc=sV)
            P.op("vector", lambda e: e.reciprocal(zrow, zrow), inc=sV)
            z_v = sV.n
            if debug and tile == 0:
                dma("sync", dOUT, dbg_z, zrow, waits=[(sV, z_v)])

            if tile + 1 < NT:
                t1 = (tile + 1) * TW
                dma("sync", sROW, rowa, ln2_dram[0:1, t1:t1 + TW],
                    waits=[(sT, bc_cur), (sRST, sRST.n)])
                dma("sync", sROW, rowb, ln2_dram[1:2, t1:t1 + TW])
                bc_n = ln_bcast(0, rowa, rowb, pb[0], pb[1],
                                extra=[(sROW, sROW.n), (sS, qact_c[KC - 1]),
                                       (sV, prev_rv)]
                                + ([(dOUT, dOUT.n)] if debug else []))
                bc_cur = bc_n

            zbc_2 = [zbc_sb, at3]
            zcopy_prev = [0]
            for c in range(KC):
                ybank = pb[2 + (c % 2)]
                yw = [(sV, z_v)]
                if debug and tile == 0 and c == 0:
                    yw.append((dOUT, dOUT.n))
                if c - 2 in ysc_at:
                    yw.append((sV, ysc_at[c - 2]))
                P.op("tensor", lambda e, c=c, ybank=ybank: e.matmul(
                    ybank[:, :], kvbd[:, c * 128:(c + 1) * 128], qf_sb(c),
                    start=True, stop=True, skip_group_check=True),
                    waits=yw, inc=sT)
                ymm = sT.n
                zbw = [(sT, ymm)]
                if zcopy_prev[0]:
                    zbw.append((sV, zcopy_prev[0]))
                elif c == 0:
                    zbw.append((sV, z_v))
                P.op("tensor", lambda e, c=c: e.matmul(
                    pb[7][:, :], ebc[:, c * 128:(c + 1) * 128], zrow,
                    start=True, stop=True, skip_group_check=True), waits=zbw, inc=sT)
                zb = sT.n
                P.op("vector", lambda e, zc=zbc_2[c % 2]: e.tensor_copy(
                    zc, pb[7][:, :]), waits=[(sT, zb)], inc=sV)
                zcopy_prev[0] = sV.n
                P.op("vector", lambda e, c=c, ybank=ybank, zc=zbc_2[c % 2]:
                     e.tensor_tensor(y_sb(c), ybank[:, :], zc, ALU.mult), inc=sV)
                ysc_at[c] = sV.n
                if tile + 1 < NT:
                    ap_t1 = (tile + 1) * TW
                    P.op("vector", lambda e, cc=c, ap_t1=ap_t1:
                         e.tensor_tensor(t1v, Xv(cc, ap_t1, TW),
                                         pb[0][:, :], ALU.mult),
                        waits=[(sT, max(bc_cur, qmm_last))], inc=sV)
                    P.op("vector", lambda e, cc=c: e.tensor_tensor(
                        ln5(cc, 0, TW), t1v, pb[1][:, :], ALU.subtract),
                        inc=sV)
                    if c == KC - 1:
                        lnr_next = sV.n
            y_v = ysc_at[KC - 1]
            if debug and tile == 0:
                dma("sync", dOUT, dbg_y, bf16v(o_ysb, KC * TW), waits=[(sV, y_v)])

            ox_at = {}
            for o in range(KC):
                bank = pb[4 + (o % 2)]
                w0 = [(sV, y_v)]
                if o - 2 in ox_at:
                    w0.append((sV, ox_at[o - 2]))
                for k in range(KC):
                    P.op("tensor", lambda e, o=o, k=k, bank=bank: e.matmul(
                        bank[:, :], w2(k * 1024 + o * 128, 128), y_sb(k),
                        start=(k == 0), stop=(k == KC - 1), skip_group_check=True),
                        waits=w0 if k == 0 else (), inc=sT if k == KC - 1 else None)
                omm = sT.n
                P.op("vector", lambda e, o=o, t0=t0, bank=bank: e.scalar_tensor_tensor(
                    Xv(o, t0, TW), bank[:, :], bo_col[:, o:o + 1], Xv(o, t0, TW),
                    ALU.add, ALU.add), waits=[(sT, omm)], inc=sV)
                ox_at[o] = sV.n
            xupd_v = sV.n
            omm_last = omm

            if tile + 1 < NT:
                qmm_last, qf_v = emit_q(lnr_next, omm_last)

            qg = [(sS, qact_c[KC - 2]), (sS, qact_c[KC - 1])]
            rv = ln_stats(t0, lambda c: [(sV, xupd_v)] + (qg if c == 0 else []),
                          axbf, ax2bf)
            prev_rv = rv
            dma("sync", sRST, ln3_dram[0:1, t0:t0 + TW], rowa, waits=[(sV, rv)])
            dma("sync", sRST, ln3_dram[1:2, t0:t0 + TW], rowb)
        attn_v = sV.n
        if debug:
            for k in range(KC):
                dma("sync", dOUT, dbg_x2[k * 128:(k + 1) * 128, :], Xv(k, 0, T),
                    waits=[(sV, attn_v)])
            dma("sync", dOUT, dbg_kv, kv_f, waits=[(sV, kvr_v)])
        attn_rst = sRST.n

        P.op("sync", lambda e: e.nop(),
             waits=[(sV, attn_v), (sRST, attn_rst), (sT, sT.n), (sS, sS.n),
                    (dOUT, dOUT.n)])
        ffn_pass(1)

        P.op("sync", lambda e: e.nop(),
             waits=[(dOUT, dOUT.n), (gOUT, gOUT.n), (sRST, sRST.n)])

        block = ctx.enter_context(nc.Block())

        @block.sync
        def _(e):
            P.replay("sync", e)

        @block.tensor
        def _(e):
            P.replay("tensor", e)

        @block.vector
        def _(e):
            P.replay("vector", e)

        @block.scalar
        def _(e):
            P.replay("scalar", e)

        @block.gpsimd
        def _(e):
            P.replay("gpsimd", e)

    return nc


_NC_CACHE = [None]


def kernel(L0_attn_x, L0_attn_Wq, L0_attn_bq, L0_attn_Wk, L0_attn_bk,
           L0_attn_Wv, L0_attn_bv, L0_attn_Wo, L0_attn_bo,
           L0_ffn_W_fc, L0_ffn_b_fc, L0_ffn_A_fc, L0_ffn_B_fc,
           L0_ffn_W_proj, L0_ffn_b_proj, L0_ffn_A_proj, L0_ffn_B_proj):
    f = np.ascontiguousarray
    x = np.asarray(L0_attn_x, dtype=np.float32)

    shared = {
        "wfc_t": f(np.asarray(L0_ffn_W_fc, np.float32).T),
        "wpj_t": f(np.asarray(L0_ffn_W_proj, np.float32).T),
        "a_fc": f(np.asarray(L0_ffn_A_fc, np.float32)),
        "b_fc_t": f(np.asarray(L0_ffn_B_fc, np.float32).T),
        "a_pj": f(np.asarray(L0_ffn_A_proj, np.float32)),
        "b_pj_t": f(np.asarray(L0_ffn_B_proj, np.float32).T),
        "wq": f(np.asarray(L0_attn_Wq, np.float32).reshape(D, D)),
        "wk": f(np.asarray(L0_attn_Wk, np.float32).reshape(D, D)),
        "wv": f(np.asarray(L0_attn_Wv, np.float32).reshape(D, D)),
        "wo": f(np.asarray(L0_attn_Wo, np.float32).reshape(D, D)),
        "bias_fc": f(np.asarray(L0_ffn_b_fc, np.float32).reshape(FC, 128)),
        "bias_pj": f(np.asarray(L0_ffn_b_proj, np.float32).reshape(KC, 128)),
        "bias_o": f(np.asarray(L0_attn_bo, np.float32).reshape(KC, 128)),
        "bias_q": f(np.asarray(L0_attn_bq, np.float32).reshape(KC, 128)),
        "bias_k": f(np.asarray(L0_attn_bk, np.float32).reshape(1, D)),
        "bias_v": f(np.asarray(L0_attn_bv, np.float32).reshape(1, D)),
        "eye": np.eye(128, dtype=np.float32),
    }
    ebc = np.zeros((H, KC * 128), dtype=np.float32)
    for c in range(KC):
        ebc[2 * c, c * 128:c * 128 + 64] = 1.0
        ebc[2 * c + 1, c * 128 + 64:(c + 1) * 128] = 1.0
    shared["e_bc"] = ebc

    if _NC_CACHE[0] is None:
        _NC_CACHE[0] = build_nc()
    nc = _NC_CACHE[0]

    in_maps = []
    for c in range(NCORES):
        b, hh = c // 2, c % 2
        m = dict(shared)
        m["x_t"] = f(x[b, hh * T:(hh + 1) * T, :].T)
        in_maps.append(m)

    res = run_bass_kernel_spmd(nc, in_maps, core_ids=list(range(NCORES)))

    out = np.empty((4, 8192, D), dtype=np.float32)
    for c in range(NCORES):
        b, hh = c // 2, c % 2
        out[b, hh * T:(hh + 1) * T, :] = res.results[c]["out_x"].T
    return out
